# version 06 (final)
# baseline (speedup 1.0000x reference)
import contextlib
import numpy as np
import ml_dtypes
import concourse.bass as bass
import concourse.mybir as mybir
from concourse.bass_utils import run_bass_kernel_spmd
from concourse.alu_op_type import AluOpType as Op

F32 = mybir.dt.float32
BF16 = mybir.dt.bfloat16
AF = mybir.ActivationFunctionType
AX = mybir.AxisListType

D = 1024
S = 2048
NH = 16
KVH = 4
HD = 64
LAT = 256
NSEQ = 4
TT = 1024
CH = D // 128
HCH = 4096 // 128
BF = ml_dtypes.bfloat16

CFG = dict(n_seq=NSEQ, ffn1=True, attn=True, ffn2=True)



def _pan(wT):
    K, O = wT.shape
    return np.ascontiguousarray(
        wT.reshape(K // 128, 128, O // 128, 128).transpose(2, 1, 0, 3))


def _host_consts():
    c = {}
    ident = np.eye(128, dtype=np.float32)
    pswap = np.zeros((128, 128), np.float32)
    for m in range(128):
        pswap[m ^ 1, m] = 1.0
    ii = np.arange(128)
    causal = (ii[:, None] >= ii[None, :]).astype(np.float32)
    upper = (ii[:, None] < ii[None, :]).astype(np.float32)
    pdup = np.zeros((4, 128, 128), np.float32)
    for cc in range(4):
        sb = 64 * (cc % 2)
        for m in range(128):
            pdup[cc][sb + (m % 64), m] = 1.0
    lmask = np.zeros((4, 128, 512), np.float32)
    for r in range(4):
        lmask[r][:, 0:r * 128] = 1.0
        lmask[r][:, r * 128:(r + 1) * 128] = causal
    llmask3 = np.concatenate(
        [upper, np.ones((128, 128), np.float32), causal], axis=1)
    hdones = np.zeros((128, 2), np.float32)
    hdones[0:64, 0] = 1.0
    hdones[64:128, 1] = 1.0
    onescol = np.ones((128, 1), np.float32)
    u = np.arange(1152) - 384
    ki_col = np.arange(128)[:, None]
    mg = (u[None, 0:896] >= ki_col).astype(np.float32)
    ml = ((u[None, :] >= ki_col) & (u[None, :] < ki_col + 256)).astype(np.float32)
    cbf = np.concatenate(
        [ident, pswap, pdup.transpose(1, 0, 2).reshape(128, 512),
         hdones, onescol, mg, ml], axis=1)
    c["cbf"] = cbf.astype(BF)
    c["o_ident"] = 0
    c["o_pswap"] = 128
    c["o_pdup"] = 256
    c["o_hdones"] = 768
    c["o_onescol"] = 770
    c["o_mg"] = 771
    c["o_ml"] = 771 + 896
    c["cbf_w"] = 771 + 896 + 1152
    selA = np.zeros(128, np.float32); selA[0:64] = 1.0
    selB = np.zeros(128, np.float32); selB[64:128] = 1.0
    crows = np.concatenate([np.ones(128, np.float32), selA, selB,
                            selA / 8.0, selB / 8.0])[None, :]
    c["crows"] = np.ascontiguousarray(crows.astype(np.float32))
    t = np.arange(S, dtype=np.float64)
    rows = []
    for base in (1e6, 1e4):
        invf = base ** (-(np.arange(0, HD, 2, dtype=np.float64)) / HD)
        th = np.outer(t, invf)
        cs, sn = np.cos(th), np.sin(th)
        C = np.zeros((128, S), np.float64)
        Sg = np.zeros((128, S), np.float64)
        for p in range(128):
            j = (p % 64) // 2
            C[p] = cs[:, j]
            Sg[p] = -sn[:, j] if (p % 64) % 2 == 0 else sn[:, j]
        rows.append(C)
        rows.append(Sg)
    c["ropetab"] = np.ascontiguousarray(np.stack(rows).astype(np.float32))
    ceps = np.zeros((128, 3), np.float32)
    ceps[:, 0] = 1e-5
    ceps[:, 1] = 1e-6
    ceps[:, 2] = -8.0
    c["ceps"] = ceps
    return c


def _host_weights(inp):
    w = {}
    w["w12p"] = np.ascontiguousarray(np.stack(
        [_pan(np.ascontiguousarray(inp["L0_ffn_w1_w"].T)),
         _pan(np.ascontiguousarray(inp["L0_ffn_w2_w"].T))], axis=2))
    w["h1p"] = _pan(np.ascontiguousarray(inp["L0_ffn_h1_w"].T))
    w["h2p"] = _pan(np.ascontiguousarray(inp["L0_ffn_h2_w"].T))
    w["w3p"] = _pan(np.ascontiguousarray(inp["L0_ffn_w3_w"].T))
    fsc = np.zeros((128, 136), np.float32)
    fsc[:, 0:32] = inp["L0_ffn_w1_s"].reshape(32, 128).T
    fsc[:, 32:64] = inp["L0_ffn_w2_s"].reshape(32, 128).T
    fsc[:, 64:96] = inp["L0_ffn_h1_s"].reshape(32, 128).T
    fsc[:, 96:128] = inp["L0_ffn_h2_s"].reshape(32, 128).T
    fsc[:, 128:136] = inp["L0_ffn_w3_s"].reshape(8, 128).T
    w["fsc"] = fsc
    anw = np.zeros((128, 16), np.float32)
    anw[:, 0:8] = inp["L0_attn_norm1_w"].reshape(8, 128).T
    anw[:, 8:16] = inp["L0_attn_norm2_w"].reshape(8, 128).T
    w["anw"] = anw
    for k_in, k_out in [("L0_attn_g_wq", "gwq"), ("L0_attn_g_kv_down", "gkvd"),
                        ("L0_attn_g_k_up", "gkup"), ("L0_attn_g_v_up", "gvup"),
                        ("L0_attn_g_wo", "gwo"), ("L0_attn_l_wq", "lwq"),
                        ("L0_attn_l_wk", "lwk"), ("L0_attn_l_wv", "lwv"),
                        ("L0_attn_l_wo", "lwo")]:
        w[k_out] = np.ascontiguousarray(inp[k_in])
    return w



class Bld:
    def __init__(self, nc, block, ctx):
        self.nc, self.block, self.ctx = nc, block, ctx
        self.s = {}
        self.c = {}

    def sem(self, name):
        if name not in self.s:
            self.s[name] = self.ctx.enter_context(self.nc.semaphore(name))
            self.c[name] = 0
        return self.s[name]

    def cur(self, name):
        self.sem(name)
        return self.c[name]

    def bump(self, name, n=1):
        self.sem(name)
        self.c[name] += n
        return self.c[name]

    def on(self, eng, fn):
        getattr(self.block, eng)(fn)

    def dma(self, eng, semname, out, in_, waits=(), **kw):
        h = self.sem(semname)
        tgt = self.bump(semname, 16)

        def f(e, out=out, in_=in_, h=h, waits=tuple(waits), kw=dict(kw)):
            for ws, wv in waits:
                if wv > 0:
                    e.wait_ge(self.s[ws], wv)
            e.dma_start(out, in_, **kw).then_inc(h, 16)
        self.on(eng, f)
        return tgt

    def barrier(self):
        snap = [(n, self.c[n]) for n in list(self.c) if self.c[n] > 0]
        for eng in ("tensor", "scalar", "vector", "gpsimd", "sync"):
            def f(e, snap=tuple(snap)):
                for n, v in snap:
                    e.wait_ge(self.s[n], v)
            self.on(eng, f)



def build_nc(cfg=CFG):
    nseq = cfg["n_seq"]
    HC = _host_consts()
    oI, oPS, oPD = HC["o_ident"], HC["o_pswap"], HC["o_pdup"]
    oHDs, oOC = HC["o_hdones"], HC["o_onescol"]
    oMG, oML = HC["o_mg"], HC["o_ml"]

    nc = bass.Bass("TRN2", target_bir_lowering=False)
    x0 = nc.dram_tensor("x0", [nseq, CH, 128, S], F32, kind="ExternalInput").ap()
    y = nc.dram_tensor("y", [nseq, CH, 128, S], F32, kind="ExternalOutput").ap()
    xscr = nc.dram_tensor("xscr", [nseq, CH, 128, S], F32).ap()
    tscr = nc.dram_tensor("tscr", [nseq, CH, 128, S], F32).ap()
    w12p = nc.dram_tensor("w12p", [32, 128, 2, 8, 128], F32, kind="ExternalInput").ap()
    h1p = nc.dram_tensor("h1p", [32, 128, 32, 128], F32, kind="ExternalInput").ap()
    h2p = nc.dram_tensor("h2p", [32, 128, 32, 128], F32, kind="ExternalInput").ap()
    w3p = nc.dram_tensor("w3p", [8, 128, 32, 128], F32, kind="ExternalInput").ap()
    fscD = nc.dram_tensor("fsc", [128, 136], F32, kind="ExternalInput").ap()
    anwD = nc.dram_tensor("anw", [128, 16], F32, kind="ExternalInput").ap()
    cbfD = nc.dram_tensor("cbf", [128, HC["cbf_w"]], BF16, kind="ExternalInput").ap()
    crowsD = nc.dram_tensor("crows", [1, 640], F32, kind="ExternalInput").ap()
    cepsD = nc.dram_tensor("ceps", [128, 3], F32, kind="ExternalInput").ap()
    ropeD = nc.dram_tensor("ropetab", [4, 128, S], F32, kind="ExternalInput").ap()
    attw = {}
    for nm, shp in [("gwq", [D, D]), ("gkvd", [D, LAT]), ("gkup", [LAT, LAT]),
                    ("gvup", [LAT, LAT]), ("gwo", [D, D]), ("lwq", [D, D]),
                    ("lwk", [D, LAT]), ("lwv", [D, LAT]), ("lwo", [D, D])]:
        attw[nm] = nc.dram_tensor(nm, shp, F32, kind="ExternalInput").ap()

    CTX = contextlib.ExitStack()
    cbf = CTX.enter_context(nc.sbuf_tensor("cbf_sb", [128, HC["cbf_w"]], BF16))
    crows = CTX.enter_context(nc.sbuf_tensor("crows_sb", [1, 640], F32))
    crows32 = CTX.enter_context(nc.sbuf_tensor("crows32_sb", [33, 128], F32))
    fsc = CTX.enter_context(nc.sbuf_tensor("fsc_sb", [128, 136], F32))
    w3sh = CTX.enter_context(nc.sbuf_tensor("w3sh", [128, 8], F32))
    anw = CTX.enter_context(nc.sbuf_tensor("anw_sb", [128, 16], F32))
    ceps = CTX.enter_context(nc.sbuf_tensor("ceps_sb", [128, 3], F32))
    block = CTX.enter_context(nc.Block())
    P = Bld(nc, block, CTX)

    P.dma("sync", "cst", cbf[:, :], cbfD)
    P.dma("sync", "cst", crows[:, :], crowsD)
    P.dma("sync", "cst", crows32[32:33, :], crowsD[0:1, 0:128])
    P.dma("sync", "cst", fsc[:, :], fscD)
    P.dma("sync", "cst", anw[:, :], anwD)
    P.dma("sync", "cst", ceps[:, :], cepsD)

    def f(v, tot=P.cur("cst")):
        v.wait_ge(P.s["cst"], tot)
        v.tensor_scalar(w3sh[:, :], fsc[:, 128:136], 0.5, None,
                        Op.mult).then_inc(P.sem("evV"), 1)
    P.on("vector", f)
    P.bump("evV")
    P.barrier()

    def ffn_phase(src, dst, tag):
        with contextlib.ExitStack() as FX:
            xT = FX.enter_context(nc.sbuf_tensor(f"xT{tag}", [128, CH, TT], F32))
            A = FX.enter_context(nc.sbuf_tensor(f"A{tag}", [128, HCH, TT], BF16))
            Bt = FX.enter_context(nc.sbuf_tensor(f"B{tag}", [128, HCH, TT], BF16))
            st1 = FX.enter_context(nc.sbuf_tensor(f"st1{tag}", [128, TT], F32))
            wS = FX.enter_context(nc.sbuf_tensor(f"wS{tag}", [128, 2, 2, 8, 128], BF16))
            wH = FX.enter_context(nc.sbuf_tensor(f"wH{tag}", [128, 2, 32, 128], BF16))
            rwA = FX.enter_context(nc.sbuf_tensor(f"rwA{tag}", [1, TT], F32))
            rwB = FX.enter_context(nc.sbuf_tensor(f"rwB{tag}", [1, TT], F32))

            pv = dict(s1mm30=0, s1mm31=0, s3mm30=0, s3mm31=0, s5mm6=0,
                      s5mm7=0, s2mm30=0, s2mm31=0, stt_last=0, s3_done=0)
            for ip, (sq, th) in enumerate([(a, b) for a in range(nseq)
                                           for b in range(2)]):
                _ffn_pass(ip, sq, th, src, dst, tag, xT, A, Bt, st1, wS, wH,
                          rwA, rwB, pv)

    def _ffn_pass(ip, sq, th, src, dst, tag, xT, A, Bt, st1, wS, wH, rwA,
                  rwB, pv):
        t0 = th * TT
        src_ap = src[sq].rearrange("c p t -> p c t")[:, :, t0:t0 + TT]
        dst_ap = dst[sq].rearrange("c p t -> p c t")[:, :, t0:t0 + TT]
        xl_t = P.dma("sync", "xl", xT[:, :, :], src_ap,
                     waits=[("xs", P.cur("xs"))])

        with contextlib.ExitStack() as PX:
            psA = PX.enter_context(nc.psum_tensor(f"psA{tag}{ip}", [1, TT], F32))
            psB = PX.enter_context(nc.psum_tensor(f"psB{tag}{ip}", [1, TT], F32))
            psBC = PX.enter_context(nc.psum_tensor(f"psBC{tag}{ip}", [128, 2, TT], F32))

            def f(v, xl_t=xl_t, pvv=pv["s3_done"]):
                v.wait_ge(P.s["xl"], xl_t)
                if pvv:
                    v.wait_ge(P.s["mmP"], pvv)
                v.tensor_copy(Bt[:, 8:16, :], xT[:, :, :]).then_inc(
                    P.sem("evV"), 1)
                v.tensor_tensor(Bt[:, 16:24, :], xT[:, :, :], xT[:, :, :],
                                Op.mult).then_inc(P.sem("evV"), 1)
            P.on("vector", f)
            sq_done = P.bump("evV", 2)

            def f(t, sq_done=sq_done):
                t.wait_ge(P.s["evV"], sq_done)
                for half in range(2):
                    for kc in range(CH):
                        t.matmul(psA[0:1, half * 512:(half + 1) * 512],
                                 cbf[:, oOC:oOC + 1],
                                 Bt[:, 8 + kc, half * 512:(half + 1) * 512],
                                 start=(kc == 0), stop=(kc == CH - 1))
                for half in range(2):
                    for kc in range(CH):
                        inst = t.matmul(
                            psB[0:1, half * 512:(half + 1) * 512],
                            cbf[:, oOC:oOC + 1],
                            Bt[:, 16 + kc, half * 512:(half + 1) * 512],
                            start=(kc == 0), stop=(kc == CH - 1))
                inst.then_inc(P.sem("mmP"), 1)
            P.on("tensor", f)
            stats_done = P.bump("mmP")

            def f(v, stats_done=stats_done, n0=P.cur("evV")):
                v.wait_ge(P.s["mmP"], stats_done)
                v.tensor_scalar(rwA[0:1, :], psA[0:1, :], 1.0 / D, None,
                                Op.mult).then_inc(P.sem("evV"), 1)
                v.wait_ge(P.s["evV"], n0 + 1)
                v.tensor_tensor(rwB[0:1, :], rwA[0:1, :], rwA[0:1, :],
                                Op.mult).then_inc(P.sem("evV"), 1)
                v.wait_ge(P.s["evV"], n0 + 2)
                v.scalar_tensor_tensor(rwB[0:1, :], psB[0:1, :], 1.0 / D,
                                       rwB[0:1, :], Op.mult,
                                       Op.subtract).then_inc(P.sem("evV"), 1)
            P.on("vector", f)
            var_done = P.bump("evV", 3)

            def f(sc, var_done=var_done, n0=P.cur("evA")):
                sc.wait_ge(P.s["evV"], var_done)
                sc.activation(rwB[0:1, :], rwB[0:1, :], AF.Ln,
                              bias=ceps[0:1, 0:1]).then_inc(P.sem("evA"), 1)
                sc.wait_ge(P.s["evA"], n0 + 1)
                sc.activation(rwB[0:1, :], rwB[0:1, :], AF.Exp,
                              scale=-0.5).then_inc(P.sem("evA"), 1)
            P.on("scalar", f)
            recip_done = P.bump("evA", 2)

            def f(t, recip_done=recip_done):
                t.wait_ge(P.s["evA"], recip_done)
                for half in range(2):
                    t.matmul(psBC[:, 0, half * 512:(half + 1) * 512],
                             crows[0:1, 0:128],
                             rwA[0:1, half * 512:(half + 1) * 512],
                             start=True, stop=True)
                for half in range(2):
                    inst = t.matmul(psBC[:, 1, half * 512:(half + 1) * 512],
                                    crows[0:1, 0:128],
                                    rwB[0:1, half * 512:(half + 1) * 512],
                                    start=True, stop=True)
                inst.then_inc(P.sem("mmP"), 1)
            P.on("tensor", f)
            bc_done = P.bump("mmP")

            def f(v, bc_done=bc_done, n0=P.cur("evV")):
                v.wait_ge(P.s["mmP"], bc_done)
                n = n0
                for cc in range(CH):
                    v.tensor_tensor(st1[:, :], xT[:, cc, :], psBC[:, 0, :],
                                    Op.subtract).then_inc(P.sem("evV"), 1)
                    n += 1
                    v.wait_ge(P.s["evV"], n)
                    v.tensor_tensor(Bt[:, cc, :], st1[:, :],
                                    psBC[:, 1, :], Op.mult).then_inc(
                        P.sem("evV"), 1)
                    n += 1
                    v.wait_ge(P.s["evV"], n)
            P.on("vector", f)
            xn_done = P.bump("evV", 2 * CH)

        s1mm, s1silu, s1stt = {}, {}, {}
        with contextlib.ExitStack() as PX:
            psS1 = PX.enter_context(
                nc.psum_tensor(f"psS1{tag}{ip}", [128, 2, 2, TT], F32))
            for oc in range(HCH):
                slot = oc % 2
                wrel = s1mm.get(oc - 2, pv["s1mm30"] if slot == 0 else pv["s1mm31"])
                ws_t = P.dma("gpsimd", f"wS{slot}", wS[:, slot], w12p[oc],
                             waits=[("mmP", wrel)], max_dma_last_dim=4096)

                def f(t, oc=oc, slot=slot, ws_t=ws_t, xn_done=xn_done,
                      sil2=s1silu.get(oc - 2, 0), stt2=s1stt.get(oc - 2, 0)):
                    t.wait_ge(P.s[f"wS{slot}"], ws_t)
                    t.wait_ge(P.s["evV"], xn_done)
                    if sil2:
                        t.wait_ge(P.s["evA"], sil2)
                    if stt2:
                        t.wait_ge(P.s["evV"], stt2)
                    for w in range(2):
                        for half in range(2):
                            for kc in range(CH):
                                inst = t.matmul(
                                    psS1[:, slot, w, half * 512:(half + 1) * 512],
                                    wS[:, slot, w, kc, :],
                                    Bt[:, kc, half * 512:(half + 1) * 512],
                                    start=(kc == 0), stop=(kc == CH - 1))
                    inst.then_inc(P.sem("mmP"), 1)
                P.on("tensor", f)
                s1mm[oc] = P.bump("mmP")

                def f(sc, oc=oc, slot=slot, mm=s1mm[oc],
                      stt1=s1stt.get(oc - 1, 0)):
                    sc.wait_ge(P.s["mmP"], mm)
                    if stt1:
                        sc.wait_ge(P.s["evV"], stt1)
                    sc.activation(st1[:, :], psS1[:, slot, 0, :], AF.Silu,
                                  scale=fsc[:, oc:oc + 1]).then_inc(
                        P.sem("evA"), 1)
                P.on("scalar", f)
                s1silu[oc] = P.bump("evA")

                def f(v, oc=oc, slot=slot, si=s1silu[oc]):
                    v.wait_ge(P.s["evA"], si)
                    v.scalar_tensor_tensor(A[:, oc, :], psS1[:, slot, 1, :],
                                           fsc[:, 32 + oc:33 + oc], st1[:, :],
                                           Op.mult, Op.mult).then_inc(
                        P.sem("evV"), 1)
                P.on("vector", f)
                s1stt[oc] = P.bump("evV")
        pv["s1mm30"], pv["s1mm31"] = s1mm[30], s1mm[31]

        def h_stage(stg, wdram, src_sb, n_oc, scol, ready_sem, ready_val,
                    slot_rel0, slot_rel1, dst_sb=None):
            mm_d, ev_d = {}, {}
            with contextlib.ExitStack() as PX:
                psD = PX.enter_context(
                    nc.psum_tensor(f"psD{tag}{ip}s{stg}", [128, 2, TT], F32))
                for oc in range(n_oc):
                    slot = oc % 2
                    wrel = mm_d.get(oc - 2, slot_rel0 if slot == 0 else slot_rel1)
                    wh_t = P.dma("gpsimd", f"wH{slot}", wH[:, slot], wdram[oc],
                                 waits=[("mmP", wrel)], max_dma_last_dim=4096)

                    def f(t, oc=oc, slot=slot, wh_t=wh_t,
                          ready_sem=ready_sem, ready_val=ready_val,
                          e2=ev_d.get(oc - 2, 0), stg=stg):
                        t.wait_ge(P.s[f"wH{slot}"], wh_t)
                        t.wait_ge(P.s[ready_sem], ready_val)
                        if e2:
                            t.wait_ge(P.s["evA" if stg in (2, 3) else "evV"], e2)
                        for half in range(2):
                            for kc in range(HCH):
                                inst = t.matmul(
                                    psD[:, slot, half * 512:(half + 1) * 512],
                                    wH[:, slot, kc, :],
                                    src_sb[:, kc, half * 512:(half + 1) * 512],
                                    start=(kc == 0), stop=(kc == HCH - 1))
                        inst.then_inc(P.sem("mmP"), 1)
                    P.on("tensor", f)
                    mm_d[oc] = P.bump("mmP")

                    if stg in (2, 3):
                        def f(sc, oc=oc, slot=slot, mm=mm_d[oc], scol=scol):
                            sc.wait_ge(P.s["mmP"], mm)
                            sc.activation(dst_sb[:, oc, :], psD[:, slot, :],
                                          AF.Gelu,
                                          scale=fsc[:, scol + oc:scol + oc + 1]
                                          ).then_inc(P.sem("evA"), 1)
                        P.on("scalar", f)
                        ev_d[oc] = P.bump("evA")
                    else:
                        def f(v, oc=oc, slot=slot, mm=mm_d[oc]):
                            v.wait_ge(P.s["mmP"], mm)
                            v.scalar_tensor_tensor(
                                xT[:, oc, :], psD[:, slot, :],
                                w3sh[:, oc:oc + 1], xT[:, oc, :],
                                Op.mult, Op.add).then_inc(P.sem("evV"), 1)
                        P.on("vector", f)
                        ev_d[oc] = P.bump("evV")
            return mm_d, ev_d

        s2mm, s2g = h_stage(2, h1p, A, HCH, 64, "evV", s1stt[31],
                            pv["s5mm6"], pv["s5mm7"], dst_sb=Bt)
        s3mm, s3g = h_stage(3, h2p, Bt, HCH, 96, "evA", s2g[31],
                            s2mm[30], s2mm[31], dst_sb=A)
        s5mm, s5stt = h_stage(5, w3p, A, 8, None, "evA", s3g[31],
                              s3mm[30], s3mm[31])
        pv["s5mm6"], pv["s5mm7"] = s5mm[6], s5mm[7]
        pv["stt_last"] = s5stt[7]
        pv["s3_done"] = s3mm[31]

        P.dma("sync", "xs", dst_ap, xT[:, :, :],
              waits=[("evV", s5stt[7])])

    def ln_phase(src, dst, tag):
        with contextlib.ExitStack() as FX:
            xT = FX.enter_context(nc.sbuf_tensor(f"xTl{tag}", [128, CH, TT], F32))
            xb = FX.enter_context(nc.sbuf_tensor(f"xbl{tag}", [128, 16, TT], BF16))
            st1 = FX.enter_context(nc.sbuf_tensor(f"st1l{tag}", [128, TT], F32))
            rwA = FX.enter_context(nc.sbuf_tensor(f"rwAl{tag}", [1, TT], F32))
            rwB = FX.enter_context(nc.sbuf_tensor(f"rwBl{tag}", [1, TT], F32))
            for ip, (sq, th) in enumerate([(a, b) for a in range(nseq)
                                           for b in range(2)]):
                t0 = th * TT
                src_ap = src[sq].rearrange("c p t -> p c t")[:, :, t0:t0 + TT]
                dst_ap = dst[sq].rearrange("c p t -> p c t")[:, :, t0:t0 + TT]
                xl_t = P.dma("sync", "xl", xT[:, :, :], src_ap,
                             waits=[("xs", P.cur("xs"))])
                with contextlib.ExitStack() as PX:
                    psA = PX.enter_context(
                        nc.psum_tensor(f"psAl{tag}{ip}", [1, TT], F32))
                    psB = PX.enter_context(
                        nc.psum_tensor(f"psBl{tag}{ip}", [1, TT], F32))
                    psBC = PX.enter_context(
                        nc.psum_tensor(f"psBCl{tag}{ip}", [128, 2, TT], F32))

                    def f(v, xl_t=xl_t):
                        v.wait_ge(P.s["xl"], xl_t)
                        v.tensor_copy(xb[:, 0:8, :], xT[:, :, :]).then_inc(
                            P.sem("evV"), 1)
                        v.tensor_tensor(xb[:, 8:16, :], xT[:, :, :],
                                        xT[:, :, :], Op.mult).then_inc(
                            P.sem("evV"), 1)
                    P.on("vector", f)
                    sq_done = P.bump("evV", 2)

                    def f(t, sq_done=sq_done):
                        t.wait_ge(P.s["evV"], sq_done)
                        for half in range(2):
                            for kc in range(CH):
                                t.matmul(psA[0:1, half * 512:(half + 1) * 512],
                                         cbf[:, oOC:oOC + 1],
                                         xb[:, kc, half * 512:(half + 1) * 512],
                                         start=(kc == 0), stop=(kc == CH - 1))
                        for half in range(2):
                            for kc in range(CH):
                                inst = t.matmul(
                                    psB[0:1, half * 512:(half + 1) * 512],
                                    cbf[:, oOC:oOC + 1],
                                    xb[:, 8 + kc, half * 512:(half + 1) * 512],
                                    start=(kc == 0), stop=(kc == CH - 1))
                        inst.then_inc(P.sem("mmP"), 1)
                    P.on("tensor", f)
                    stats_done = P.bump("mmP")

                    def f(v, stats_done=stats_done, n0=P.cur("evV")):
                        v.wait_ge(P.s["mmP"], stats_done)
                        v.tensor_scalar(rwA[0:1, :], psA[0:1, :], 1.0 / D,
                                        None, Op.mult).then_inc(P.sem("evV"), 1)
                        v.wait_ge(P.s["evV"], n0 + 1)
                        v.tensor_tensor(rwB[0:1, :], rwA[0:1, :], rwA[0:1, :],
                                        Op.mult).then_inc(P.sem("evV"), 1)
                        v.wait_ge(P.s["evV"], n0 + 2)
                        v.scalar_tensor_tensor(rwB[0:1, :], psB[0:1, :],
                                               1.0 / D, rwB[0:1, :], Op.mult,
                                               Op.subtract).then_inc(
                            P.sem("evV"), 1)
                    P.on("vector", f)
                    var_done = P.bump("evV", 3)

                    def f(sc, var_done=var_done, n0=P.cur("evA")):
                        sc.wait_ge(P.s["evV"], var_done)
                        sc.activation(rwB[0:1, :], rwB[0:1, :], AF.Ln,
                                      bias=ceps[0:1, 0:1]).then_inc(
                            P.sem("evA"), 1)
                        sc.wait_ge(P.s["evA"], n0 + 1)
                        sc.activation(rwB[0:1, :], rwB[0:1, :], AF.Exp,
                                      scale=-0.5).then_inc(P.sem("evA"), 1)
                    P.on("scalar", f)
                    recip_done = P.bump("evA", 2)

                    def f(t, recip_done=recip_done):
                        t.wait_ge(P.s["evA"], recip_done)
                        for half in range(2):
                            t.matmul(psBC[:, 0, half * 512:(half + 1) * 512],
                                     crows[0:1, 0:128],
                                     rwA[0:1, half * 512:(half + 1) * 512],
                                     start=True, stop=True)
                        for half in range(2):
                            inst = t.matmul(
                                psBC[:, 1, half * 512:(half + 1) * 512],
                                crows[0:1, 0:128],
                                rwB[0:1, half * 512:(half + 1) * 512],
                                start=True, stop=True)
                        inst.then_inc(P.sem("mmP"), 1)
                    P.on("tensor", f)
                    bc_done = P.bump("mmP")

                    def f(v, bc_done=bc_done, n0=P.cur("evV")):
                        v.wait_ge(P.s["mmP"], bc_done)
                        n = n0
                        for cc in range(CH):
                            v.tensor_tensor(st1[:, :], xT[:, cc, :],
                                            psBC[:, 0, :],
                                            Op.subtract).then_inc(
                                P.sem("evV"), 1)
                            n += 1
                            v.wait_ge(P.s["evV"], n)
                            v.tensor_tensor(xT[:, cc, :], st1[:, :],
                                            psBC[:, 1, :], Op.mult).then_inc(
                                P.sem("evV"), 1)
                            n += 1
                            v.wait_ge(P.s["evV"], n)
                    P.on("vector", f)
                    xn_done = P.bump("evV", 2 * CH)

                P.dma("sync", "xs", dst_ap, xT[:, :, :],
                      waits=[("evV", xn_done)])

    def rope_qknorm(scr, sqb, pre_sb, pre_idx, rpt, out_sb, out_idx, is_q,
                    rwA, rwB, pre_done, tag, rpt_t=0):
        with contextlib.ExitStack() as PX:
            psSw = PX.enter_context(
                nc.psum_tensor(f"psSw{tag}", [128, 2, 512], F32))
            sw_d = {}
            for h4 in range(4):
                slot = h4 % 2

                def f(t, h4=h4, slot=slot, pre_done=pre_done,
                      e2=sw_d.get(h4 - 2, 0)):
                    t.wait_ge(P.s["evV"], pre_done)
                    if e2:
                        t.wait_ge(P.s["evV"], e2)
                    t.matmul(psSw[:, slot, :], cbf[:, oPS:oPS + 128],
                             pre_sb[:, pre_idx, h4 * 512:(h4 + 1) * 512],
                             start=True, stop=True).then_inc(P.sem("mmP"), 1)
                P.on("tensor", f)
                mm = P.bump("mmP")

                def f(v, mm=mm, slot=slot, h4=h4, rpt_t=rpt_t):
                    v.wait_ge(P.s["mmP"], mm)
                    if rpt_t:
                        v.wait_ge(P.s["xl"], rpt_t)
                    v.tensor_tensor(scr[:, 1, h4 * 512:(h4 + 1) * 512],
                                    psSw[:, slot, :],
                                    rpt[:, 1, h4 * 512:(h4 + 1) * 512],
                                    Op.mult).then_inc(P.sem("evV"), 1)
                P.on("vector", f)
                sw_d[h4] = P.bump("evV")

            def f(v, sd=sw_d[3], n0=P.cur("evV")):
                v.wait_ge(P.s["evV"], sd)
                v.tensor_tensor(scr[:, 0, :], scr[:, 0, :], rpt[:, 0, :],
                                Op.mult).then_inc(P.sem("evV"), 1)
                v.wait_ge(P.s["evV"], n0 + 1)
                v.tensor_tensor(scr[:, 0, :], scr[:, 0, :], scr[:, 1, :],
                                Op.add).then_inc(P.sem("evV"), 1)
                v.wait_ge(P.s["evV"], n0 + 2)
                v.tensor_tensor(sqb[:, 0, :], scr[:, 0, :], scr[:, 0, :],
                                Op.mult).then_inc(P.sem("evV"), 1)
            P.on("vector", f)
            roped = P.bump("evV", 3)

        with contextlib.ExitStack() as PX:
            psR = PX.enter_context(nc.psum_tensor(f"psR{tag}", [1, S], F32))
            row_done = 0
            for hh, row in ((0, rwA), (1, rwB)):
                def f(t, hh=hh, roped=roped, row_done=row_done):
                    t.wait_ge(P.s["evV"], roped)
                    if row_done:
                        t.wait_ge(P.s["evV"], row_done)
                    for q4 in range(4):
                        inst = t.matmul(psR[0:1, q4 * 512:(q4 + 1) * 512],
                                        cbf[:, oHDs + hh:oHDs + hh + 1],
                                        sqb[:, 0, q4 * 512:(q4 + 1) * 512],
                                        start=True, stop=True)
                    inst.then_inc(P.sem("mmP"), 1)
                P.on("tensor", f)
                mm = P.bump("mmP")

                def f(v, mm=mm, row=row):
                    v.wait_ge(P.s["mmP"], mm)
                    v.tensor_scalar(row[0:1, :], psR[0:1, :], 1.0 / HD, None,
                                    Op.mult).then_inc(P.sem("evV"), 1)
                P.on("vector", f)
                row_done = P.bump("evV")

            def f(sc, row_done=row_done, n0=P.cur("evA")):
                sc.wait_ge(P.s["evV"], row_done)
                sc.activation(rwA[0:1, :], rwA[0:1, :], AF.Ln,
                              bias=ceps[0:1, 1:2]).then_inc(P.sem("evA"), 1)
                sc.activation(rwB[0:1, :], rwB[0:1, :], AF.Ln,
                              bias=ceps[0:1, 1:2]).then_inc(P.sem("evA"), 1)
                sc.wait_ge(P.s["evA"], n0 + 2)
                sc.activation(rwA[0:1, :], rwA[0:1, :], AF.Exp,
                              scale=-0.5).then_inc(P.sem("evA"), 1)
                sc.activation(rwB[0:1, :], rwB[0:1, :], AF.Exp,
                              scale=-0.5).then_inc(P.sem("evA"), 1)
            P.on("scalar", f)
            rr = P.bump("evA", 4)

        with contextlib.ExitStack() as PX:
            psBC = PX.enter_context(nc.psum_tensor(f"psBCq{tag}", [128, S], F32))
            selo = 384 if is_q else 128

            def f(t, rr=rr, selo=selo):
                t.wait_ge(P.s["evA"], rr)
                for q4 in range(4):
                    t.matmul(psBC[:, q4 * 512:(q4 + 1) * 512],
                             crows[0:1, selo:selo + 128],
                             rwA[0:1, q4 * 512:(q4 + 1) * 512],
                             start=True, stop=False)
                    inst = t.matmul(psBC[:, q4 * 512:(q4 + 1) * 512],
                                    crows[0:1, selo + 128:selo + 256],
                                    rwB[0:1, q4 * 512:(q4 + 1) * 512],
                                    start=False, stop=True)
                inst.then_inc(P.sem("mmP"), 1)
            P.on("tensor", f)
            bc2 = P.bump("mmP")

            def f(v, bc2=bc2):
                v.wait_ge(P.s["mmP"], bc2)
                v.tensor_tensor(out_sb[:, out_idx, :], scr[:, 0, :],
                                psBC[:, :], Op.mult).then_inc(P.sem("evV"), 1)
            P.on("vector", f)
            P.bump("evV")

    def sdpa_all(g, Q, KLD, VT, AO, Pstr, Zsb, Zall, rZ, psSc, psO, psZ,
                 psZbc, ready0):
        metas = []
        gctr = 0
        for hp in range(NH // 2):
            for jq in range(4):
                if g:
                    kts = list(range(0, 4 * (jq + 1)))
                else:
                    kts = list(range(max(0, 4 * jq - 2), 4 * jq + 4))
                groups = [kts[i:i + 2] for i in range(0, len(kts), 2)]
                ntot = len(kts)
                for gi, grp in enumerate(groups):
                    for h in (2 * hp, 2 * hp + 1):
                        moffs = []
                        for kt in grp:
                            d = kt - 4 * jq
                            if g:
                                moffs.append(oMG + 384 - 128 * d if d >= 0
                                             else None)
                            else:
                                moffs.append(oML + 384 - 128 * d)
                        metas.append(dict(
                            h=h, jq=jq, b0=64 * (h % 2), kvh=h // 4,
                            qch=h // 2, grp=tuple(grp), slot=gctr % 2,
                            moffs=tuple(moffs), ki0=2 * gi, ntot=ntot,
                            oslot=h % 2, zp=32 * (h % 2),
                            first=(gi == 0),
                            last=(2 * gi + len(grp) == ntot)))
                        gctr += 1
        NG = len(metas)
        expd = [0] * NG
        qkd = [0] * NG
        sd = dict(zraw={0: 0, 1: 0}, lnprev={0: 0, 1: 0}, zbcwar=0,
                  po={0: 0, 1: 0})
        pending = []

        def emit_qk(i):
            m = metas[i]
            exp_rel = expd[i - 2] if i >= 2 else 0
            rdy = ready0 if i == 0 else 0

            def f(t, m=m, exp_rel=exp_rel, rdy=rdy):
                if rdy:
                    t.wait_ge(P.s["evV"], rdy)
                if exp_rel:
                    t.wait_ge(P.s["evA"], exp_rel)
                qsl = Q[m["b0"]:m["b0"] + 64, m["qch"],
                        m["jq"] * 512:(m["jq"] + 1) * 512]
                for j, kt in enumerate(m["grp"]):
                    inst = t.matmul(psSc[:, m["slot"], j, :],
                                    KLD[m["b0"]:m["b0"] + 64, m["kvh"],
                                        kt * 128:(kt + 1) * 128],
                                    qsl, start=True, stop=True)
                inst.then_inc(P.sem("mmP"), 1)
            P.on("tensor", f)
            qkd[i] = P.bump("mmP")

        def make_head_tail(m, rec):
            def emit():
                zp = m["zp"]
                for j4 in range(4):
                    def f(t, rec=rec, j4=j4, war=sd["zbcwar"], zp=zp):
                        t.wait_ge(P.s["evA"], rec)
                        if war:
                            t.wait_ge(P.s["evV"], war)
                        if zp:
                            t.matmul(psZbc[:, :], crows32[32:33, 0:128],
                                     rZ[32:33, j4 * 512:(j4 + 1) * 512],
                                     start=True, stop=True).then_inc(
                                P.sem("mmP"), 1)
                        else:
                            t.matmul(psZbc[:, :], crows[0:1, 0:128],
                                     rZ[0:1, j4 * 512:(j4 + 1) * 512],
                                     start=True, stop=True).then_inc(
                                P.sem("mmP"), 1)
                    P.on("tensor", f)
                    bc = P.bump("mmP")

                    def f(v, bc=bc, b0=m["b0"], j4=j4):
                        v.wait_ge(P.s["mmP"], bc)
                        v.tensor_copy(Zsb[b0:b0 + 64, j4 % 2, :],
                                      psZbc[b0:b0 + 64, :]).then_inc(
                            P.sem("evV"), 1)
                    P.on("vector", f)
                    sd["zbcwar"] = P.bump("evV")

                    def f(v, zc=sd["zbcwar"], b0=m["b0"], qch=m["qch"],
                          j4=j4):
                        v.wait_ge(P.s["evV"], zc)
                        v.tensor_tensor(
                            AO[b0:b0 + 64, qch, j4 * 512:(j4 + 1) * 512],
                            AO[b0:b0 + 64, qch, j4 * 512:(j4 + 1) * 512],
                            Zsb[b0:b0 + 64, j4 % 2, :],
                            Op.mult).then_inc(P.sem("evV"), 1)
                    P.on("vector", f)
                    P.bump("evV")
            return emit

        emit_qk(0)
        for i in range(NG):
            m = metas[i]

            def f(sc, m=m, qk=qkd[i]):
                sc.wait_ge(P.s["mmP"], qk)
                sc.activation(Pstr[:, m["slot"], :], psSc[:, m["slot"], :, :],
                              AF.Exp, bias=ceps[:, 2:3]).then_inc(
                    P.sem("evA"), 1)
            P.on("scalar", f)
            expd[i] = P.bump("evA")

            if i + 1 < NG:
                emit_qk(i + 1)

            nmask = sum(1 for mo in m["moffs"] if mo is not None)
            if nmask:
                def f(v, m=m, e=expd[i]):
                    v.wait_ge(P.s["evA"], e)
                    for j, moff in enumerate(m["moffs"]):
                        if moff is None:
                            continue
                        v.scalar_tensor_tensor(
                            Pstr[:, m["slot"], j * 512:(j + 1) * 512],
                            Pstr[:, m["slot"], j * 512:(j + 1) * 512], 1.0,
                            cbf[:, moff:moff + 512], Op.mult,
                            Op.mult).then_inc(P.sem("evV"), 1)
                P.on("vector", f)
                prod = ("evV", P.bump("evV", nmask))
            else:
                prod = ("evA", expd[i])

            def f(t, m=m, prod=prod,
                  zwar=(sd["zraw"][m["oslot"]] if m["first"] else 0),
                  powar=(sd["po"][m["oslot"]] if m["first"] else 0)):
                t.wait_ge(P.s[prod[0]], prod[1])
                if zwar:
                    t.wait_ge(P.s["evV"], zwar)
                if powar:
                    t.wait_ge(P.s["evV"], powar)
                zp = m["zp"]
                for j in range(len(m["grp"])):
                    if zp:
                        t.matmul(psZ[32:33, :], cbf[:, oOC:oOC + 1],
                                 Pstr[:, m["slot"], j * 512:(j + 1) * 512],
                                 start=(m["ki0"] + j == 0),
                                 stop=(m["ki0"] + j == m["ntot"] - 1),
                                 tile_position=(0, 32))
                    else:
                        t.matmul(psZ[0:1, :], cbf[:, oOC:oOC + 1],
                                 Pstr[:, m["slot"], j * 512:(j + 1) * 512],
                                 start=(m["ki0"] + j == 0),
                                 stop=(m["ki0"] + j == m["ntot"] - 1))
                for j, kt in enumerate(m["grp"]):
                    inst = t.matmul(
                        psO[m["b0"]:m["b0"] + 64, m["oslot"], :],
                        VT[:, kt, m["kvh"] * 64:(m["kvh"] + 1) * 64],
                        Pstr[:, m["slot"], j * 512:(j + 1) * 512],
                        start=(m["ki0"] + j == 0),
                        stop=(m["ki0"] + j == m["ntot"] - 1),
                        tile_position=(0, m["b0"]))
                inst.then_inc(P.sem("mmP"), 1)
            P.on("tensor", f)
            pvz = P.bump("mmP")

            if m["last"]:
                def f(v, m=m, pvz=pvz,
                      lnw=(sd["lnprev"][m["oslot"]] if m["jq"] == 0 else 0)):
                    v.wait_ge(P.s["mmP"], pvz)
                    v.tensor_copy(
                        AO[m["b0"]:m["b0"] + 64, m["qch"],
                           m["jq"] * 512:(m["jq"] + 1) * 512],
                        psO[m["b0"]:m["b0"] + 64, m["oslot"], :]).then_inc(
                        P.sem("evV"), 1)
                    if lnw:
                        v.wait_ge(P.s["evA"], lnw)
                    v.tensor_copy(
                        Zall[m["zp"]:m["zp"] + 1,
                             m["jq"] * 512:(m["jq"] + 1) * 512],
                        psZ[m["zp"]:m["zp"] + 1, :]).then_inc(P.sem("evV"), 1)
                P.on("vector", f)
                sd["po"][m["oslot"]] = P.bump("evV")
                sd["zraw"][m["oslot"]] = P.bump("evV")

                if m["jq"] == 3:
                    def f(sc, zraw=sd["zraw"][m["oslot"]], zbw=sd["zbcwar"],
                          zp=m["zp"], n0=P.cur("evA")):
                        sc.wait_ge(P.s["evV"], zraw)
                        if zbw:
                            sc.wait_ge(P.s["evV"], zbw)
                        sc.activation(rZ[zp:zp + 1, :], Zall[zp:zp + 1, :],
                                      AF.Ln).then_inc(P.sem("evA"), 1)
                        sc.wait_ge(P.s["evA"], n0 + 1)
                        sc.activation(rZ[zp:zp + 1, :], rZ[zp:zp + 1, :],
                                      AF.Exp, scale=-1.0).then_inc(
                            P.sem("evA"), 1)
                    P.on("scalar", f)
                    rec = P.bump("evA", 2)
                    sd["lnprev"][m["oslot"]] = rec - 1
                    pending.append([i + 4, make_head_tail(m, rec)])

            for item in list(pending):
                if item[0] <= i:
                    item[1]()
                    pending.remove(item)

        for item in pending:
            item[1]()

    def attn_sublayer(sq, kind):
        g = kind == "g"
        wq_d = attw["gwq" if g else "lwq"]
        wo_d = attw["gwo" if g else "lwo"]
        kv_d = attw["gkvd" if g else "lwk"]
        nrm = 0 if g else 8
        rbase = 0 if g else 2
        res_src = tscr if g else xscr
        res_dst = tscr if g else xscr
        add_t = not g

        with contextlib.ExitStack() as SX:
            Q = SX.enter_context(nc.sbuf_tensor(f"Q{sq}{kind}", [128, CH, S], BF16))
            VT = SX.enter_context(nc.sbuf_tensor(f"VT{sq}{kind}", [128, 16, 256], BF16))
            KLD = SX.enter_context(nc.sbuf_tensor(f"KLD{sq}{kind}", [128, 4, S], BF16))

            with contextlib.ExitStack() as S1:
                XN = S1.enter_context(nc.sbuf_tensor(f"XN{sq}{kind}", [128, CH, S], BF16))
                KL = S1.enter_context(nc.sbuf_tensor(f"KL{sq}{kind}", [128, 2, S], BF16))
                LATB = S1.enter_context(nc.sbuf_tensor(f"LATB{sq}{kind}", [128, 2, S], BF16))
                wq_a = S1.enter_context(nc.sbuf_tensor(f"wqa{sq}{kind}", [128, 8, D], BF16))
                kvd_a = S1.enter_context(nc.sbuf_tensor(f"kvda{sq}{kind}", [128, 8, 256], BF16))
                wv_a = S1.enter_context(nc.sbuf_tensor(f"wva{sq}{kind}", [128, 8, 256], BF16))
                scr = S1.enter_context(nc.sbuf_tensor(f"scr{sq}{kind}", [128, 2, S], F32))
                sqb = S1.enter_context(nc.sbuf_tensor(f"sqb{sq}{kind}", [128, 2, S], BF16))
                rwA = S1.enter_context(nc.sbuf_tensor(f"rwAa{sq}{kind}", [1, S], F32))
                rwB = S1.enter_context(nc.sbuf_tensor(f"rwBa{sq}{kind}", [1, S], F32))
                rpt = S1.enter_context(nc.sbuf_tensor(f"rpt{sq}{kind}", [128, 2, S], F32))

                rpt_t = P.dma("sync", "xl", rpt[:, :, :],
                              ropeD[rbase:rbase + 2].rearrange("r p t -> p r t"))
                P.dma("gpsimd", "aw", wq_a[:, :, :],
                      wq_d.rearrange("(kc p) o -> p kc o", p=128),
                      max_dma_last_dim=4096)
                P.dma("gpsimd", "aw", kvd_a[:, :, :],
                      kv_d.rearrange("(kc p) o -> p kc o", p=128))
                if g:
                    P.dma("gpsimd", "aw", wv_a[:, 0:2, :],
                          attw["gkup"].rearrange("(kc p) o -> p kc o", p=128))
                    P.dma("gpsimd", "aw", wv_a[:, 2:4, :],
                          attw["gvup"].rearrange("(kc p) o -> p kc o", p=128))
                else:
                    P.dma("gpsimd", "aw", wv_a[:, :, :],
                          attw["lwv"].rearrange("(kc p) o -> p kc o", p=128))
                aw_t = P.cur("aw")

                with contextlib.ExitStack() as PX:
                    psB = PX.enter_context(nc.psum_tensor(f"psBr{sq}{kind}", [1, S], F32))
                    sq_d = {}
                    st_d = {}
                    for cc in range(CH):
                        sl = f"xlA{cc % 2}"
                        xl_t = P.dma("sync", sl, scr[:, cc % 2, :],
                                     tscr[sq, cc],
                                     waits=[("xs", P.cur("xs")),
                                            ("evV", sq_d.get(cc - 2, 0))])

                        def f(v, cc=cc, xl_t=xl_t, sl=sl,
                              sr=st_d.get(cc - 2, 0)):
                            v.wait_ge(P.s[sl], xl_t)
                            if sr:
                                v.wait_ge(P.s["mmP"], sr)
                            v.tensor_tensor(sqb[:, cc % 2, :], scr[:, cc % 2, :],
                                            scr[:, cc % 2, :],
                                            Op.mult).then_inc(P.sem("evV"), 1)
                        P.on("vector", f)
                        sq_d[cc] = P.bump("evV")

                        def f(t, cc=cc, sd=sq_d[cc]):
                            t.wait_ge(P.s["evV"], sd)
                            for q4 in range(4):
                                inst = t.matmul(
                                    psB[0:1, q4 * 512:(q4 + 1) * 512],
                                    cbf[:, oOC:oOC + 1],
                                    sqb[:, cc % 2, q4 * 512:(q4 + 1) * 512],
                                    start=(cc == 0), stop=(cc == CH - 1))
                            inst.then_inc(P.sem("mmP"), 1)
                        P.on("tensor", f)
                        st_d[cc] = P.bump("mmP")
                    stats_done = st_d[CH - 1]

                    def f(v, stats_done=stats_done):
                        v.wait_ge(P.s["mmP"], stats_done)
                        v.tensor_scalar(rwB[0:1, :], psB[0:1, :], 1.0 / D,
                                        None, Op.mult).then_inc(P.sem("evV"), 1)
                    P.on("vector", f)
                    msd = P.bump("evV")

                    def f(sc, msd=msd, n0=P.cur("evA")):
                        sc.wait_ge(P.s["evV"], msd)
                        sc.activation(rwB[0:1, :], rwB[0:1, :], AF.Ln,
                                      bias=ceps[0:1, 1:2]).then_inc(P.sem("evA"), 1)
                        sc.wait_ge(P.s["evA"], n0 + 1)
                        sc.activation(rwB[0:1, :], rwB[0:1, :], AF.Exp,
                                      scale=-0.5).then_inc(P.sem("evA"), 1)
                    P.on("scalar", f)
                    rstd_done = P.bump("evA", 2)

                with contextlib.ExitStack() as PX:
                    psBC = PX.enter_context(
                        nc.psum_tensor(f"psBCr{sq}{kind}", [128, S], F32))

                    def f(t, rstd_done=rstd_done):
                        t.wait_ge(P.s["evA"], rstd_done)
                        for q4 in range(4):
                            inst = t.matmul(psBC[:, q4 * 512:(q4 + 1) * 512],
                                            crows[0:1, 0:128],
                                            rwB[0:1, q4 * 512:(q4 + 1) * 512],
                                            start=True, stop=True)
                        inst.then_inc(P.sem("mmP"), 1)
                    P.on("tensor", f)
                    bcd = P.bump("mmP")

                    xn_d = {}
                    for cc in range(CH):
                        sl = f"xlA{cc % 2}"
                        xl_t = P.dma("sync", sl, scr[:, cc % 2, :],
                                     tscr[sq, cc],
                                     waits=[("evV", xn_d.get(cc - 2, 0) or
                                             P.cur("evV"))])

                        def f(v, cc=cc, xl_t=xl_t, bcd=bcd, sl=sl):
                            v.wait_ge(P.s[sl], xl_t)
                            v.wait_ge(P.s["mmP"], bcd)
                            v.scalar_tensor_tensor(
                                XN[:, cc, :], scr[:, cc % 2, :],
                                anw[:, nrm + cc:nrm + cc + 1], psBC[:, :],
                                Op.mult, Op.mult).then_inc(P.sem("evV"), 1)
                        P.on("vector", f)
                        xn_d[cc] = P.bump("evV")
                    xn_done = xn_d[CH - 1]

                with contextlib.ExitStack() as PX:
                    psP = PX.enter_context(
                        nc.psum_tensor(f"psP{sq}{kind}", [128, 2, 512], F32))
                    pg = dict(g=0, rel={0: 0, 1: 0})

                    def proj_group(lhs_fn, rhs_fn, nk, out_sl, extra=0):
                        slot = pg["g"] % 2
                        pg["g"] += 1

                        def f(t, slot=slot, nk=nk, lhs_fn=lhs_fn,
                              rhs_fn=rhs_fn, aw_t=aw_t, xn_done=xn_done,
                              e2=pg["rel"][slot], extra=extra):
                            t.wait_ge(P.s["aw"], aw_t)
                            t.wait_ge(P.s["evV"], xn_done)
                            if extra:
                                t.wait_ge(P.s["evA"], extra)
                            if e2:
                                t.wait_ge(P.s["evA"], e2)
                            for k in range(nk):
                                inst = t.matmul(psP[:, slot, :], lhs_fn(k),
                                                rhs_fn(k), start=(k == 0),
                                                stop=(k == nk - 1))
                            inst.then_inc(P.sem("mmP"), 1)
                        P.on("tensor", f)
                        mm = P.bump("mmP")

                        def f(sc, mm=mm, slot=slot, out_sl=out_sl,
                              evv=P.cur("evV")):
                            sc.wait_ge(P.s["mmP"], mm)
                            sc.wait_ge(P.s["evV"], evv)
                            sc.activation(out_sl, psP[:, slot, :],
                                          AF.Copy).then_inc(P.sem("evA"), 1)
                        P.on("scalar", f)
                        ev = P.bump("evA")
                        pg["rel"][slot] = ev
                        return ev

                    lat_done = 0
                    if g:
                        for oc2 in range(2):
                            for h4 in range(4):
                                lat_done = proj_group(
                                    lambda k, oc2=oc2: kvd_a[:, k, oc2 * 128:(oc2 + 1) * 128],
                                    lambda k, h4=h4: XN[:, k, h4 * 512:(h4 + 1) * 512],
                                    CH, LATB[:, oc2, h4 * 512:(h4 + 1) * 512])

                    for lc in range(2):
                        for h4 in range(4):
                            if g:
                                kev = proj_group(
                                    lambda k, lc=lc: wv_a[:, k, lc * 128:(lc + 1) * 128],
                                    lambda k, h4=h4: LATB[:, k, h4 * 512:(h4 + 1) * 512],
                                    2, scr[:, 0, h4 * 512:(h4 + 1) * 512],
                                    extra=lat_done)
                            else:
                                kev = proj_group(
                                    lambda k, lc=lc: kvd_a[:, k, lc * 128:(lc + 1) * 128],
                                    lambda k, h4=h4: XN[:, k, h4 * 512:(h4 + 1) * 512],
                                    CH, scr[:, 0, h4 * 512:(h4 + 1) * 512])

                        def f(v, kev=kev, lc=lc):
                            v.wait_ge(P.s["evA"], kev)
                            v.tensor_copy(KL[:, lc, :], scr[:, 0, :]).then_inc(
                                P.sem("evV"), 1)
                        P.on("vector", f)
                        pre_done = P.bump("evV")
                        rope_qknorm(scr, sqb, KL, lc, rpt, KL, lc, False,
                                    rwA, rwB, pre_done, f"k{sq}{kind}{lc}",
                                    rpt_t=rpt_t)

                    with contextlib.ExitStack() as P2:
                        psV = P2.enter_context(
                            nc.psum_tensor(f"psV{sq}{kind}", [128, 2, 512], F32))
                        v_d = {}
                        for tt_ in range(16):
                            slot = tt_ % 2

                            def f(t, tt_=tt_, slot=slot, lat_done=lat_done,
                                  aw_t=aw_t, e2=v_d.get(tt_ - 2, 0),
                                  evv0=P.cur("evV")):
                                if lat_done:
                                    t.wait_ge(P.s["evA"], lat_done)
                                t.wait_ge(P.s["evV"], evv0)
                                t.wait_ge(P.s["aw"], aw_t)
                                if e2:
                                    t.wait_ge(P.s["evV"], e2)
                                if g:
                                    for k in range(2):
                                        inst = t.matmul(
                                            psV[:, slot, 0:256],
                                            LATB[:, k, tt_ * 128:(tt_ + 1) * 128],
                                            wv_a[:, 2 + k, :],
                                            start=(k == 0), stop=(k == 1))
                                else:
                                    for k in range(CH):
                                        inst = t.matmul(
                                            psV[:, slot, 0:256],
                                            XN[:, k, tt_ * 128:(tt_ + 1) * 128],
                                            wv_a[:, k, :],
                                            start=(k == 0), stop=(k == CH - 1))
                                inst.then_inc(P.sem("mmP"), 1)
                            P.on("tensor", f)
                            mm = P.bump("mmP")

                            def f(v, mm=mm, slot=slot, tt_=tt_):
                                v.wait_ge(P.s["mmP"], mm)
                                v.tensor_copy(VT[:, tt_, :],
                                              psV[:, slot, 0:256]).then_inc(
                                    P.sem("evV"), 1)
                            P.on("vector", f)
                            v_d[tt_] = P.bump("evV")

                    for qc in range(CH):
                        for h4 in range(4):
                            qev = proj_group(
                                lambda k, qc=qc: wq_a[:, k, qc * 128:(qc + 1) * 128],
                                lambda k, h4=h4: XN[:, k, h4 * 512:(h4 + 1) * 512],
                                CH, scr[:, 0, h4 * 512:(h4 + 1) * 512])

                        def f(v, qev=qev, qc=qc):
                            v.wait_ge(P.s["evA"], qev)
                            v.tensor_copy(Q[:, qc, :], scr[:, 0, :]).then_inc(
                                P.sem("evV"), 1)
                        P.on("vector", f)
                        pre_done = P.bump("evV")
                        rope_qknorm(scr, sqb, Q, qc, rpt, Q, qc, True,
                                    rwA, rwB, pre_done, f"q{sq}{kind}{qc}",
                                    rpt_t=rpt_t)

                with contextlib.ExitStack() as P2:
                    psDp = P2.enter_context(
                        nc.psum_tensor(f"psDp{sq}{kind}", [128, 2, 512], F32))
                    d_rel = {0: 0, 1: 0}
                    gi2 = 0
                    for c4 in range(4):
                        for h4 in range(4):
                            slot = gi2 % 2
                            gi2 += 1

                            def f(t, c4=c4, h4=h4, slot=slot,
                                  e2=d_rel[slot], pe=P.cur("evV")):
                                t.wait_ge(P.s["evV"], pe)
                                if e2:
                                    t.wait_ge(P.s["evV"], e2)
                                t.matmul(psDp[:, slot, :],
                                         cbf[:, oPD + c4 * 128:oPD + (c4 + 1) * 128],
                                         KL[:, c4 // 2, h4 * 512:(h4 + 1) * 512],
                                         start=True, stop=True).then_inc(
                                    P.sem("mmP"), 1)
                            P.on("tensor", f)
                            mm = P.bump("mmP")

                            def f(v, mm=mm, slot=slot, c4=c4, h4=h4):
                                v.wait_ge(P.s["mmP"], mm)
                                v.tensor_copy(
                                    KLD[:, c4, h4 * 512:(h4 + 1) * 512],
                                    psDp[:, slot, :]).then_inc(P.sem("evV"), 1)
                            P.on("vector", f)
                            d_rel[slot] = P.bump("evV")

            P.barrier()

            with contextlib.ExitStack() as S2:
                AO = S2.enter_context(nc.sbuf_tensor(f"AO{sq}{kind}", [128, CH, S], BF16))
                wo_a = S2.enter_context(nc.sbuf_tensor(f"woa{sq}{kind}", [128, 8, D], BF16))
                Pstr = S2.enter_context(nc.sbuf_tensor(f"Pstr{sq}{kind}", [128, 2, 1024], BF16))
                Zsb = S2.enter_context(nc.sbuf_tensor(f"Zsb{sq}{kind}", [128, 2, 512], F32))
                Zall = S2.enter_context(nc.sbuf_tensor(f"Zall{sq}{kind}", [33, 2048], F32))
                rZ = S2.enter_context(nc.sbuf_tensor(f"rZ{sq}{kind}", [33, 2048], F32))
                xc2 = S2.enter_context(nc.sbuf_tensor(f"xc2{sq}{kind}", [128, 2, S], F32))
                if add_t:
                    tc2 = S2.enter_context(nc.sbuf_tensor(f"tc2{sq}{kind}", [128, 2, S], F32))

                P.dma("gpsimd", "aw", wo_a[:, :, :],
                      wo_d.rearrange("(kc p) o -> p kc o", p=128),
                      max_dma_last_dim=4096)
                wo_t = P.cur("aw")

                with contextlib.ExitStack() as P2:
                    psSc = P2.enter_context(
                        nc.psum_tensor(f"psSc{sq}{kind}", [128, 2, 2, 512], F32))
                    psO = P2.enter_context(
                        nc.psum_tensor(f"psO{sq}{kind}", [128, 2, 512], F32))
                    psZ = P2.enter_context(
                        nc.psum_tensor(f"psZ{sq}{kind}", [33, 512], F32))
                    psZbc = P2.enter_context(
                        nc.psum_tensor(f"psZbc{sq}{kind}", [128, 512], F32))
                    sdpa_all(g, Q, KLD, VT, AO, Pstr, Zsb, Zall, rZ,
                             psSc, psO, psZ, psZbc, P.cur("evV"))

                ao_done = P.cur("evV")
                with contextlib.ExitStack() as P2:
                    psP2 = P2.enter_context(
                        nc.psum_tensor(f"psP2{sq}{kind}", [128, 2, 512], F32))
                    orel = {0: 0, 1: 0}
                    tadd_d = {}
                    gi3 = 0
                    for oc in range(CH):
                        sl = f"xlB{oc % 2}"
                        xl_t = P.dma("sync", sl, xc2[:, oc % 2, :],
                                     res_src[sq, oc],
                                     waits=[("xs", P.cur("xs")),
                                            ("evV", orel[oc % 2])])
                        if add_t:
                            sl2 = f"xlC{oc % 2}"
                            t2_t = P.dma("gpsimd", sl2, tc2[:, oc % 2, :],
                                         tscr[sq, oc],
                                         waits=[("xs", P.cur("xs")),
                                                ("evV", tadd_d.get(oc - 2, 0))])

                            def f(v, oc=oc, xl_t=xl_t, t2_t=t2_t, sl=sl,
                                  sl2=sl2):
                                v.wait_ge(P.s[sl], xl_t)
                                v.wait_ge(P.s[sl2], t2_t)
                                v.tensor_tensor(xc2[:, oc % 2, :],
                                                xc2[:, oc % 2, :],
                                                tc2[:, oc % 2, :],
                                                Op.add).then_inc(P.sem("evV"), 1)
                            P.on("vector", f)
                            tadd_d[oc] = P.bump("evV")
                            base_ev = tadd_d[oc]
                        else:
                            base_ev = 0
                        res_ev = 0
                        for h4 in range(4):
                            slot = gi3 % 2
                            gi3 += 1

                            def f(t, oc=oc, h4=h4, slot=slot, wo_t=wo_t,
                                  ao_done=ao_done, e2=orel[slot]):
                                t.wait_ge(P.s["aw"], wo_t)
                                t.wait_ge(P.s["evV"], ao_done)
                                if e2:
                                    t.wait_ge(P.s["evV"], e2)
                                for k in range(CH):
                                    inst = t.matmul(
                                        psP2[:, slot, :],
                                        wo_a[:, k, oc * 128:(oc + 1) * 128],
                                        AO[:, k, h4 * 512:(h4 + 1) * 512],
                                        start=(k == 0), stop=(k == CH - 1))
                                inst.then_inc(P.sem("mmP"), 1)
                            P.on("tensor", f)
                            mm = P.bump("mmP")

                            def f(v, mm=mm, slot=slot, oc=oc, h4=h4,
                                  xl_t=xl_t, sl=sl, base_ev=base_ev):
                                v.wait_ge(P.s["mmP"], mm)
                                v.wait_ge(P.s[sl], xl_t)
                                if base_ev:
                                    v.wait_ge(P.s["evV"], base_ev)
                                v.tensor_tensor(
                                    xc2[:, oc % 2, h4 * 512:(h4 + 1) * 512],
                                    psP2[:, slot, :],
                                    xc2[:, oc % 2, h4 * 512:(h4 + 1) * 512],
                                    Op.add).then_inc(P.sem("evV"), 1)
                            P.on("vector", f)
                            res_ev = P.bump("evV")
                            orel[slot] = res_ev
                        P.dma("sync", "xs", res_dst[sq, oc], xc2[:, oc % 2, :],
                              waits=[("evV", res_ev)])
            P.barrier()

    if cfg["ffn1"]:
        ffn_phase(x0, xscr, "f1")
        P.barrier()
    else:
        for sq in range(nseq):
            for cc in range(CH):
                P.dma("sync", "xs", xscr[sq, cc], x0[sq, cc])
        P.barrier()
    if cfg["attn"]:
        ln_phase(xscr, tscr, "ln")
        P.barrier()
        for sq in range(nseq):
            attn_sublayer(sq, "g")
            attn_sublayer(sq, "l")
    if cfg["ffn2"]:
        ffn_phase(xscr, y, "f2")
        snap = [(n, P.c[n]) for n in list(P.c) if P.c[n] > 0]

        def f(e, snap=tuple(snap)):
            for n, v in snap:
                e.wait_ge(P.s[n], v)
        P.on("sync", f)
    else:
        for sq in range(nseq):
            for cc in range(CH):
                P.dma("sync", "ys", y[sq, cc], xscr[sq, cc],
                      waits=[("xs", P.cur("xs"))])

        def f(e, t=P.cur("ys")):
            e.wait_ge(P.s["ys"], t)
        P.on("sync", f)

    CTX.close()
    return nc



_NC_CACHE = {}


def _get_nc():
    if "nc" not in _NC_CACHE:
        _NC_CACHE["nc"] = build_nc(CFG)
    return _NC_CACHE["nc"]


def kernel(**inputs):
    hidden = inputs["L0_attn_hidden"]
    W = _host_weights(inputs)
    C = _host_consts()
    nc = _get_nc()

    shared = dict(
        w12p=W["w12p"], h1p=W["h1p"], h2p=W["h2p"], w3p=W["w3p"],
        fsc=W["fsc"], anw=W["anw"], cbf=C["cbf"], crows=C["crows"],
        ropetab=C["ropetab"], ceps=C["ceps"], gwq=W["gwq"], gkvd=W["gkvd"], gkup=W["gkup"],
        gvup=W["gvup"], gwo=W["gwo"], lwq=W["lwq"], lwk=W["lwk"],
        lwv=W["lwv"], lwo=W["lwo"])

    in_maps = []
    for c in range(8):
        sl = hidden[c * NSEQ:(c + 1) * NSEQ]
        xfm = np.ascontiguousarray(sl.transpose(0, 2, 1)).reshape(
            NSEQ, CH, 128, S)
        m = dict(shared)
        m["x0"] = xfm
        in_maps.append(m)

    res = run_bass_kernel_spmd(nc, in_maps, core_ids=list(range(8)))
    outs = []
    for c in range(8):
        yc = res.results[c]["y"].reshape(NSEQ, D, S)
        outs.append(yc.transpose(0, 2, 1))
    return np.ascontiguousarray(np.concatenate(outs, axis=0))


# version 08
# speedup vs baseline: 1.0251x; 1.0251x over previous
import contextlib
import numpy as np
import ml_dtypes
import concourse.bass as bass
import concourse.mybir as mybir
from concourse.bass_utils import run_bass_kernel_spmd
from concourse.alu_op_type import AluOpType as Op

F32 = mybir.dt.float32
BF16 = mybir.dt.bfloat16
AF = mybir.ActivationFunctionType
AX = mybir.AxisListType

D = 1024
S = 2048
NH = 16
KVH = 4
HD = 64
LAT = 256
NSEQ = 4
TT = 1024
CH = D // 128
HCH = 4096 // 128
BF = ml_dtypes.bfloat16

CFG = dict(n_seq=NSEQ, ffn1=True, attn=True, ffn2=True)



def _pan(wT):
    K, O = wT.shape
    return np.ascontiguousarray(
        wT.reshape(K // 128, 128, O // 128, 128).transpose(2, 1, 0, 3))


def _host_consts():
    c = {}
    ident = np.eye(128, dtype=np.float32)
    pswap = np.zeros((128, 128), np.float32)
    for m in range(128):
        pswap[m ^ 1, m] = 1.0
    ii = np.arange(128)
    causal = (ii[:, None] >= ii[None, :]).astype(np.float32)
    upper = (ii[:, None] < ii[None, :]).astype(np.float32)
    pdup = np.zeros((4, 128, 128), np.float32)
    for cc in range(4):
        sb = 64 * (cc % 2)
        for m in range(128):
            pdup[cc][sb + (m % 64), m] = 1.0
    lmask = np.zeros((4, 128, 512), np.float32)
    for r in range(4):
        lmask[r][:, 0:r * 128] = 1.0
        lmask[r][:, r * 128:(r + 1) * 128] = causal
    llmask3 = np.concatenate(
        [upper, np.ones((128, 128), np.float32), causal], axis=1)
    hdones = np.zeros((128, 2), np.float32)
    hdones[0:64, 0] = 1.0
    hdones[64:128, 1] = 1.0
    onescol = np.ones((128, 1), np.float32)
    u = np.arange(1152) - 384
    ki_col = np.arange(128)[:, None]
    mg = (u[None, 0:896] >= ki_col).astype(np.float32)
    ml = ((u[None, :] >= ki_col) & (u[None, :] < ki_col + 256)).astype(np.float32)
    cbf = np.concatenate(
        [ident, pswap, pdup.transpose(1, 0, 2).reshape(128, 512),
         hdones, onescol, mg, ml], axis=1)
    c["cbf"] = cbf.astype(BF)
    c["o_ident"] = 0
    c["o_pswap"] = 128
    c["o_pdup"] = 256
    c["o_hdones"] = 768
    c["o_onescol"] = 770
    c["o_mg"] = 771
    c["o_ml"] = 771 + 896
    c["cbf_w"] = 771 + 896 + 1152
    selA = np.zeros(128, np.float32); selA[0:64] = 1.0
    selB = np.zeros(128, np.float32); selB[64:128] = 1.0
    crows = np.concatenate([np.ones(128, np.float32), selA, selB,
                            selA / 8.0, selB / 8.0])[None, :]
    c["crows"] = np.ascontiguousarray(crows.astype(np.float32))
    t = np.arange(S, dtype=np.float64)
    rows = []
    for base in (1e6, 1e4):
        invf = base ** (-(np.arange(0, HD, 2, dtype=np.float64)) / HD)
        th = np.outer(t, invf)
        cs, sn = np.cos(th), np.sin(th)
        C = np.zeros((128, S), np.float64)
        Sg = np.zeros((128, S), np.float64)
        for p in range(128):
            j = (p % 64) // 2
            C[p] = cs[:, j]
            Sg[p] = -sn[:, j] if (p % 64) % 2 == 0 else sn[:, j]
        rows.append(C)
        rows.append(Sg)
    c["ropetab"] = np.ascontiguousarray(np.stack(rows).astype(np.float32))
    ceps = np.zeros((128, 3), np.float32)
    ceps[:, 0] = 1e-5
    ceps[:, 1] = 1e-6
    ceps[:, 2] = -8.0
    c["ceps"] = ceps
    return c


def _host_weights(inp):
    w = {}
    w["w12p"] = np.ascontiguousarray(np.stack(
        [_pan(np.ascontiguousarray(inp["L0_ffn_w1_w"].T)),
         _pan(np.ascontiguousarray(inp["L0_ffn_w2_w"].T))], axis=2))
    w["h1p"] = _pan(np.ascontiguousarray(inp["L0_ffn_h1_w"].T))
    w["h2p"] = _pan(np.ascontiguousarray(inp["L0_ffn_h2_w"].T))
    w["w3p"] = _pan(np.ascontiguousarray(inp["L0_ffn_w3_w"].T))
    fsc = np.zeros((128, 136), np.float32)
    fsc[:, 0:32] = inp["L0_ffn_w1_s"].reshape(32, 128).T
    fsc[:, 32:64] = inp["L0_ffn_w2_s"].reshape(32, 128).T
    fsc[:, 64:96] = inp["L0_ffn_h1_s"].reshape(32, 128).T
    fsc[:, 96:128] = inp["L0_ffn_h2_s"].reshape(32, 128).T
    fsc[:, 128:136] = inp["L0_ffn_w3_s"].reshape(8, 128).T
    w["fsc"] = fsc
    anw = np.zeros((128, 16), np.float32)
    anw[:, 0:8] = inp["L0_attn_norm1_w"].reshape(8, 128).T
    anw[:, 8:16] = inp["L0_attn_norm2_w"].reshape(8, 128).T
    w["anw"] = anw
    for k_in, k_out in [("L0_attn_g_wq", "gwq"), ("L0_attn_g_kv_down", "gkvd"),
                        ("L0_attn_g_k_up", "gkup"), ("L0_attn_g_v_up", "gvup"),
                        ("L0_attn_g_wo", "gwo"), ("L0_attn_l_wq", "lwq"),
                        ("L0_attn_l_wk", "lwk"), ("L0_attn_l_wv", "lwv"),
                        ("L0_attn_l_wo", "lwo")]:
        w[k_out] = np.ascontiguousarray(inp[k_in])
    return w



class Bld:
    def __init__(self, nc, block, ctx):
        self.nc, self.block, self.ctx = nc, block, ctx
        self.s = {}
        self.c = {}

    def sem(self, name):
        if name not in self.s:
            self.s[name] = self.ctx.enter_context(self.nc.semaphore(name))
            self.c[name] = 0
        return self.s[name]

    def cur(self, name):
        self.sem(name)
        return self.c[name]

    def bump(self, name, n=1):
        self.sem(name)
        self.c[name] += n
        return self.c[name]

    def on(self, eng, fn):
        getattr(self.block, eng)(fn)

    def dma(self, eng, semname, out, in_, waits=(), **kw):
        h = self.sem(semname)
        tgt = self.bump(semname, 16)

        def f(e, out=out, in_=in_, h=h, waits=tuple(waits), kw=dict(kw)):
            for ws, wv in waits:
                if wv > 0:
                    e.wait_ge(self.s[ws], wv)
            e.dma_start(out, in_, **kw).then_inc(h, 16)
        self.on(eng, f)
        return tgt

    def barrier(self):
        snap = [(n, self.c[n]) for n in list(self.c) if self.c[n] > 0]
        for eng in ("tensor", "scalar", "vector", "gpsimd", "sync"):
            def f(e, snap=tuple(snap)):
                for n, v in snap:
                    e.wait_ge(self.s[n], v)
            self.on(eng, f)



def build_nc(cfg=CFG):
    nseq = cfg["n_seq"]
    HC = _host_consts()
    oI, oPS, oPD = HC["o_ident"], HC["o_pswap"], HC["o_pdup"]
    oHDs, oOC = HC["o_hdones"], HC["o_onescol"]
    oMG, oML = HC["o_mg"], HC["o_ml"]

    nc = bass.Bass("TRN2", target_bir_lowering=False)
    x0 = nc.dram_tensor("x0", [nseq, CH, 128, S], F32, kind="ExternalInput").ap()
    y = nc.dram_tensor("y", [nseq, CH, 128, S], F32, kind="ExternalOutput").ap()
    xscr = nc.dram_tensor("xscr", [nseq, CH, 128, S], F32).ap()
    tscr = nc.dram_tensor("tscr", [nseq, CH, 128, S], F32).ap()
    w12p = nc.dram_tensor("w12p", [32, 128, 2, 8, 128], F32, kind="ExternalInput").ap()
    h1p = nc.dram_tensor("h1p", [32, 128, 32, 128], F32, kind="ExternalInput").ap()
    h2p = nc.dram_tensor("h2p", [32, 128, 32, 128], F32, kind="ExternalInput").ap()
    w3p = nc.dram_tensor("w3p", [8, 128, 32, 128], F32, kind="ExternalInput").ap()
    fscD = nc.dram_tensor("fsc", [128, 136], F32, kind="ExternalInput").ap()
    anwD = nc.dram_tensor("anw", [128, 16], F32, kind="ExternalInput").ap()
    cbfD = nc.dram_tensor("cbf", [128, HC["cbf_w"]], BF16, kind="ExternalInput").ap()
    crowsD = nc.dram_tensor("crows", [1, 640], F32, kind="ExternalInput").ap()
    cepsD = nc.dram_tensor("ceps", [128, 3], F32, kind="ExternalInput").ap()
    ropeD = nc.dram_tensor("ropetab", [4, 128, S], F32, kind="ExternalInput").ap()
    attw = {}
    for nm, shp in [("gwq", [D, D]), ("gkvd", [D, LAT]), ("gkup", [LAT, LAT]),
                    ("gvup", [LAT, LAT]), ("gwo", [D, D]), ("lwq", [D, D]),
                    ("lwk", [D, LAT]), ("lwv", [D, LAT]), ("lwo", [D, D])]:
        attw[nm] = nc.dram_tensor(nm, shp, F32, kind="ExternalInput").ap()

    CTX = contextlib.ExitStack()
    cbf = CTX.enter_context(nc.sbuf_tensor("cbf_sb", [128, HC["cbf_w"]], BF16))
    crows = CTX.enter_context(nc.sbuf_tensor("crows_sb", [1, 640], F32))
    crows32 = CTX.enter_context(nc.sbuf_tensor("crows32_sb", [33, 128], F32))
    fsc = CTX.enter_context(nc.sbuf_tensor("fsc_sb", [128, 136], F32))
    w3sh = CTX.enter_context(nc.sbuf_tensor("w3sh", [128, 8], F32))
    anw = CTX.enter_context(nc.sbuf_tensor("anw_sb", [128, 16], F32))
    ceps = CTX.enter_context(nc.sbuf_tensor("ceps_sb", [128, 3], F32))
    block = CTX.enter_context(nc.Block())
    P = Bld(nc, block, CTX)

    P.dma("sync", "cst", cbf[:, :], cbfD)
    P.dma("sync", "cst", crows[:, :], crowsD)
    P.dma("sync", "cst", crows32[32:33, :], crowsD[0:1, 0:128])
    P.dma("sync", "cst", fsc[:, :], fscD)
    P.dma("sync", "cst", anw[:, :], anwD)
    P.dma("sync", "cst", ceps[:, :], cepsD)

    def f(v, tot=P.cur("cst")):
        v.wait_ge(P.s["cst"], tot)
        v.tensor_scalar(w3sh[:, :], fsc[:, 128:136], 0.5, None,
                        Op.mult).then_inc(P.sem("evV"), 1)
    P.on("vector", f)
    P.bump("evV")
    P.barrier()

    def ffn_phase(src, dst, tag):
        with contextlib.ExitStack() as FX:
            xT = FX.enter_context(nc.sbuf_tensor(f"xT{tag}", [128, CH, TT], F32))
            A = FX.enter_context(nc.sbuf_tensor(f"A{tag}", [128, HCH, TT], BF16))
            Bt = FX.enter_context(nc.sbuf_tensor(f"B{tag}", [128, HCH, TT], BF16))
            st1 = FX.enter_context(nc.sbuf_tensor(f"st1{tag}", [128, TT], F32))
            wS = FX.enter_context(nc.sbuf_tensor(f"wS{tag}", [128, 2, 2, 8, 128], BF16))
            wH = FX.enter_context(nc.sbuf_tensor(f"wH{tag}", [128, 2, 32, 128], BF16))
            rwA = FX.enter_context(nc.sbuf_tensor(f"rwA{tag}", [1, TT], F32))
            rwB = FX.enter_context(nc.sbuf_tensor(f"rwB{tag}", [1, TT], F32))

            pv = dict(s1mm30=0, s1mm31=0, s3mm30=0, s3mm31=0, s5mm6=0,
                      s5mm7=0, s2mm30=0, s2mm31=0, stt_last=0, s3_done=0)
            for ip, (sq, th) in enumerate([(a, b) for a in range(nseq)
                                           for b in range(2)]):
                _ffn_pass(ip, sq, th, src, dst, tag, xT, A, Bt, st1, wS, wH,
                          rwA, rwB, pv)

    def _ffn_pass(ip, sq, th, src, dst, tag, xT, A, Bt, st1, wS, wH, rwA,
                  rwB, pv):
        t0 = th * TT
        src_ap = src[sq].rearrange("c p t -> p c t")[:, :, t0:t0 + TT]
        dst_ap = dst[sq].rearrange("c p t -> p c t")[:, :, t0:t0 + TT]
        xl_t = P.dma("sync", "xl", xT[:, :, :], src_ap,
                     waits=[("xs", P.cur("xs"))])

        with contextlib.ExitStack() as PX:
            psA = PX.enter_context(nc.psum_tensor(f"psA{tag}{ip}", [1, TT], F32))
            psB = PX.enter_context(nc.psum_tensor(f"psB{tag}{ip}", [1, TT], F32))
            psBC = PX.enter_context(nc.psum_tensor(f"psBC{tag}{ip}", [128, 2, TT], F32))

            def f(v, xl_t=xl_t, pvv=pv["s3_done"]):
                v.wait_ge(P.s["xl"], xl_t)
                if pvv:
                    v.wait_ge(P.s["mmP"], pvv)
                v.tensor_copy(Bt[:, 8:16, :], xT[:, :, :]).then_inc(
                    P.sem("evV"), 1)
                v.tensor_tensor(Bt[:, 16:24, :], xT[:, :, :], xT[:, :, :],
                                Op.mult).then_inc(P.sem("evV"), 1)
            P.on("vector", f)
            sq_done = P.bump("evV", 2)

            def f(t, sq_done=sq_done):
                t.wait_ge(P.s["evV"], sq_done)
                for half in range(2):
                    for kc in range(CH):
                        t.matmul(psA[0:1, half * 512:(half + 1) * 512],
                                 cbf[:, oOC:oOC + 1],
                                 Bt[:, 8 + kc, half * 512:(half + 1) * 512],
                                 start=(kc == 0), stop=(kc == CH - 1))
                for half in range(2):
                    for kc in range(CH):
                        inst = t.matmul(
                            psB[0:1, half * 512:(half + 1) * 512],
                            cbf[:, oOC:oOC + 1],
                            Bt[:, 16 + kc, half * 512:(half + 1) * 512],
                            start=(kc == 0), stop=(kc == CH - 1))
                inst.then_inc(P.sem("mmP"), 1)
            P.on("tensor", f)
            stats_done = P.bump("mmP")

            def f(v, stats_done=stats_done, n0=P.cur("evV")):
                v.wait_ge(P.s["mmP"], stats_done)
                v.tensor_scalar(rwA[0:1, :], psA[0:1, :], 1.0 / D, None,
                                Op.mult).then_inc(P.sem("evV"), 1)
                v.wait_ge(P.s["evV"], n0 + 1)
                v.tensor_tensor(rwB[0:1, :], rwA[0:1, :], rwA[0:1, :],
                                Op.mult).then_inc(P.sem("evV"), 1)
                v.wait_ge(P.s["evV"], n0 + 2)
                v.scalar_tensor_tensor(rwB[0:1, :], psB[0:1, :], 1.0 / D,
                                       rwB[0:1, :], Op.mult,
                                       Op.subtract).then_inc(P.sem("evV"), 1)
            P.on("vector", f)
            var_done = P.bump("evV", 3)

            def f(sc, var_done=var_done, n0=P.cur("evA")):
                sc.wait_ge(P.s["evV"], var_done)
                sc.activation(rwB[0:1, :], rwB[0:1, :], AF.Ln,
                              bias=ceps[0:1, 0:1]).then_inc(P.sem("evA"), 1)
                sc.wait_ge(P.s["evA"], n0 + 1)
                sc.activation(rwB[0:1, :], rwB[0:1, :], AF.Exp,
                              scale=-0.5).then_inc(P.sem("evA"), 1)
            P.on("scalar", f)
            recip_done = P.bump("evA", 2)

            def f(t, recip_done=recip_done):
                t.wait_ge(P.s["evA"], recip_done)
                for half in range(2):
                    t.matmul(psBC[:, 0, half * 512:(half + 1) * 512],
                             crows[0:1, 0:128],
                             rwA[0:1, half * 512:(half + 1) * 512],
                             start=True, stop=True)
                for half in range(2):
                    inst = t.matmul(psBC[:, 1, half * 512:(half + 1) * 512],
                                    crows[0:1, 0:128],
                                    rwB[0:1, half * 512:(half + 1) * 512],
                                    start=True, stop=True)
                inst.then_inc(P.sem("mmP"), 1)
            P.on("tensor", f)
            bc_done = P.bump("mmP")

            def f(v, bc_done=bc_done, n0=P.cur("evV")):
                v.wait_ge(P.s["mmP"], bc_done)
                n = n0
                for cc in range(CH):
                    v.tensor_tensor(st1[:, :], xT[:, cc, :], psBC[:, 0, :],
                                    Op.subtract).then_inc(P.sem("evV"), 1)
                    n += 1
                    v.wait_ge(P.s["evV"], n)
                    v.tensor_tensor(Bt[:, cc, :], st1[:, :],
                                    psBC[:, 1, :], Op.mult).then_inc(
                        P.sem("evV"), 1)
                    n += 1
                    v.wait_ge(P.s["evV"], n)
            P.on("vector", f)
            xn_done = P.bump("evV", 2 * CH)

        s1mm, s1silu, s1stt = {}, {}, {}
        with contextlib.ExitStack() as PX:
            psS1 = PX.enter_context(
                nc.psum_tensor(f"psS1{tag}{ip}", [128, 2, 2, TT], F32))
            for oc in range(HCH):
                slot = oc % 2
                wrel = s1mm.get(oc - 2, pv["s1mm30"] if slot == 0 else pv["s1mm31"])
                ws_t = P.dma("gpsimd", f"wS{slot}", wS[:, slot], w12p[oc],
                             waits=[("mmP", wrel)], max_dma_last_dim=4096)

                def f(t, oc=oc, slot=slot, ws_t=ws_t, xn_done=xn_done,
                      sil2=s1silu.get(oc - 2, 0), stt2=s1stt.get(oc - 2, 0)):
                    t.wait_ge(P.s[f"wS{slot}"], ws_t)
                    t.wait_ge(P.s["evV"], xn_done)
                    if sil2:
                        t.wait_ge(P.s["evA"], sil2)
                    if stt2:
                        t.wait_ge(P.s["evV"], stt2)
                    for w in range(2):
                        for half in range(2):
                            for kc in range(CH):
                                inst = t.matmul(
                                    psS1[:, slot, w, half * 512:(half + 1) * 512],
                                    wS[:, slot, w, kc, :],
                                    Bt[:, kc, half * 512:(half + 1) * 512],
                                    start=(kc == 0), stop=(kc == CH - 1))
                    inst.then_inc(P.sem("mmP"), 1)
                P.on("tensor", f)
                s1mm[oc] = P.bump("mmP")

                def f(sc, oc=oc, slot=slot, mm=s1mm[oc],
                      stt1=s1stt.get(oc - 1, 0)):
                    sc.wait_ge(P.s["mmP"], mm)
                    if stt1:
                        sc.wait_ge(P.s["evV"], stt1)
                    sc.activation(st1[:, :], psS1[:, slot, 0, :], AF.Silu,
                                  scale=fsc[:, oc:oc + 1]).then_inc(
                        P.sem("evA"), 1)
                P.on("scalar", f)
                s1silu[oc] = P.bump("evA")

                def f(v, oc=oc, slot=slot, si=s1silu[oc]):
                    v.wait_ge(P.s["evA"], si)
                    v.scalar_tensor_tensor(A[:, oc, :], psS1[:, slot, 1, :],
                                           fsc[:, 32 + oc:33 + oc], st1[:, :],
                                           Op.mult, Op.mult).then_inc(
                        P.sem("evV"), 1)
                P.on("vector", f)
                s1stt[oc] = P.bump("evV")
        pv["s1mm30"], pv["s1mm31"] = s1mm[30], s1mm[31]

        def h_stage(stg, wdram, src_sb, n_oc, scol, ready_sem, ready_val,
                    slot_rel0, slot_rel1, dst_sb=None):
            mm_d, ev_d = {}, {}
            with contextlib.ExitStack() as PX:
                psD = PX.enter_context(
                    nc.psum_tensor(f"psD{tag}{ip}s{stg}", [128, 2, TT], F32))
                for oc in range(n_oc):
                    slot = oc % 2
                    wrel = mm_d.get(oc - 2, slot_rel0 if slot == 0 else slot_rel1)
                    wh_t = P.dma("gpsimd", f"wH{slot}", wH[:, slot], wdram[oc],
                                 waits=[("mmP", wrel)], max_dma_last_dim=4096)

                    def f(t, oc=oc, slot=slot, wh_t=wh_t,
                          ready_sem=ready_sem, ready_val=ready_val,
                          e2=ev_d.get(oc - 2, 0), stg=stg):
                        t.wait_ge(P.s[f"wH{slot}"], wh_t)
                        t.wait_ge(P.s[ready_sem], ready_val)
                        if e2:
                            t.wait_ge(P.s["evA" if stg in (2, 3) else "evV"], e2)
                        for half in range(2):
                            for kc in range(HCH):
                                inst = t.matmul(
                                    psD[:, slot, half * 512:(half + 1) * 512],
                                    wH[:, slot, kc, :],
                                    src_sb[:, kc, half * 512:(half + 1) * 512],
                                    start=(kc == 0), stop=(kc == HCH - 1))
                        inst.then_inc(P.sem("mmP"), 1)
                    P.on("tensor", f)
                    mm_d[oc] = P.bump("mmP")

                    if stg in (2, 3):
                        def f(sc, oc=oc, slot=slot, mm=mm_d[oc], scol=scol):
                            sc.wait_ge(P.s["mmP"], mm)
                            sc.activation(dst_sb[:, oc, :], psD[:, slot, :],
                                          AF.Gelu,
                                          scale=fsc[:, scol + oc:scol + oc + 1]
                                          ).then_inc(P.sem("evA"), 1)
                        P.on("scalar", f)
                        ev_d[oc] = P.bump("evA")
                    else:
                        def f(v, oc=oc, slot=slot, mm=mm_d[oc]):
                            v.wait_ge(P.s["mmP"], mm)
                            v.scalar_tensor_tensor(
                                xT[:, oc, :], psD[:, slot, :],
                                w3sh[:, oc:oc + 1], xT[:, oc, :],
                                Op.mult, Op.add).then_inc(P.sem("evV"), 1)
                        P.on("vector", f)
                        ev_d[oc] = P.bump("evV")
            return mm_d, ev_d

        s2mm, s2g = h_stage(2, h1p, A, HCH, 64, "evV", s1stt[31],
                            pv["s5mm6"], pv["s5mm7"], dst_sb=Bt)
        s3mm, s3g = h_stage(3, h2p, Bt, HCH, 96, "evA", s2g[31],
                            s2mm[30], s2mm[31], dst_sb=A)
        s5mm, s5stt = h_stage(5, w3p, A, 8, None, "evA", s3g[31],
                              s3mm[30], s3mm[31])
        pv["s5mm6"], pv["s5mm7"] = s5mm[6], s5mm[7]
        pv["stt_last"] = s5stt[7]
        pv["s3_done"] = s3mm[31]

        P.dma("sync", "xs", dst_ap, xT[:, :, :],
              waits=[("evV", s5stt[7])])

    def ln_phase(src, dst, tag):
        with contextlib.ExitStack() as FX:
            xT = FX.enter_context(nc.sbuf_tensor(f"xTl{tag}", [128, CH, TT], F32))
            xb = FX.enter_context(nc.sbuf_tensor(f"xbl{tag}", [128, 16, TT], BF16))
            st1 = FX.enter_context(nc.sbuf_tensor(f"st1l{tag}", [128, TT], F32))
            rwA = FX.enter_context(nc.sbuf_tensor(f"rwAl{tag}", [1, TT], F32))
            rwB = FX.enter_context(nc.sbuf_tensor(f"rwBl{tag}", [1, TT], F32))
            for ip, (sq, th) in enumerate([(a, b) for a in range(nseq)
                                           for b in range(2)]):
                t0 = th * TT
                src_ap = src[sq].rearrange("c p t -> p c t")[:, :, t0:t0 + TT]
                dst_ap = dst[sq].rearrange("c p t -> p c t")[:, :, t0:t0 + TT]
                xl_t = P.dma("sync", "xl", xT[:, :, :], src_ap,
                             waits=[("xs", P.cur("xs"))])
                with contextlib.ExitStack() as PX:
                    psA = PX.enter_context(
                        nc.psum_tensor(f"psAl{tag}{ip}", [1, TT], F32))
                    psB = PX.enter_context(
                        nc.psum_tensor(f"psBl{tag}{ip}", [1, TT], F32))
                    psBC = PX.enter_context(
                        nc.psum_tensor(f"psBCl{tag}{ip}", [128, 2, TT], F32))

                    def f(v, xl_t=xl_t):
                        v.wait_ge(P.s["xl"], xl_t)
                        v.tensor_copy(xb[:, 0:8, :], xT[:, :, :]).then_inc(
                            P.sem("evV"), 1)
                        v.tensor_tensor(xb[:, 8:16, :], xT[:, :, :],
                                        xT[:, :, :], Op.mult).then_inc(
                            P.sem("evV"), 1)
                    P.on("vector", f)
                    sq_done = P.bump("evV", 2)

                    def f(t, sq_done=sq_done):
                        t.wait_ge(P.s["evV"], sq_done)
                        for half in range(2):
                            for kc in range(CH):
                                t.matmul(psA[0:1, half * 512:(half + 1) * 512],
                                         cbf[:, oOC:oOC + 1],
                                         xb[:, kc, half * 512:(half + 1) * 512],
                                         start=(kc == 0), stop=(kc == CH - 1))
                        for half in range(2):
                            for kc in range(CH):
                                inst = t.matmul(
                                    psB[0:1, half * 512:(half + 1) * 512],
                                    cbf[:, oOC:oOC + 1],
                                    xb[:, 8 + kc, half * 512:(half + 1) * 512],
                                    start=(kc == 0), stop=(kc == CH - 1))
                        inst.then_inc(P.sem("mmP"), 1)
                    P.on("tensor", f)
                    stats_done = P.bump("mmP")

                    def f(v, stats_done=stats_done, n0=P.cur("evV")):
                        v.wait_ge(P.s["mmP"], stats_done)
                        v.tensor_scalar(rwA[0:1, :], psA[0:1, :], 1.0 / D,
                                        None, Op.mult).then_inc(P.sem("evV"), 1)
                        v.wait_ge(P.s["evV"], n0 + 1)
                        v.tensor_tensor(rwB[0:1, :], rwA[0:1, :], rwA[0:1, :],
                                        Op.mult).then_inc(P.sem("evV"), 1)
                        v.wait_ge(P.s["evV"], n0 + 2)
                        v.scalar_tensor_tensor(rwB[0:1, :], psB[0:1, :],
                                               1.0 / D, rwB[0:1, :], Op.mult,
                                               Op.subtract).then_inc(
                            P.sem("evV"), 1)
                    P.on("vector", f)
                    var_done = P.bump("evV", 3)

                    def f(sc, var_done=var_done, n0=P.cur("evA")):
                        sc.wait_ge(P.s["evV"], var_done)
                        sc.activation(rwB[0:1, :], rwB[0:1, :], AF.Ln,
                                      bias=ceps[0:1, 0:1]).then_inc(
                            P.sem("evA"), 1)
                        sc.wait_ge(P.s["evA"], n0 + 1)
                        sc.activation(rwB[0:1, :], rwB[0:1, :], AF.Exp,
                                      scale=-0.5).then_inc(P.sem("evA"), 1)
                    P.on("scalar", f)
                    recip_done = P.bump("evA", 2)

                    def f(t, recip_done=recip_done):
                        t.wait_ge(P.s["evA"], recip_done)
                        for half in range(2):
                            t.matmul(psBC[:, 0, half * 512:(half + 1) * 512],
                                     crows[0:1, 0:128],
                                     rwA[0:1, half * 512:(half + 1) * 512],
                                     start=True, stop=True)
                        for half in range(2):
                            inst = t.matmul(
                                psBC[:, 1, half * 512:(half + 1) * 512],
                                crows[0:1, 0:128],
                                rwB[0:1, half * 512:(half + 1) * 512],
                                start=True, stop=True)
                        inst.then_inc(P.sem("mmP"), 1)
                    P.on("tensor", f)
                    bc_done = P.bump("mmP")

                    def f(v, bc_done=bc_done, n0=P.cur("evV")):
                        v.wait_ge(P.s["mmP"], bc_done)
                        n = n0
                        for cc in range(CH):
                            v.tensor_tensor(st1[:, :], xT[:, cc, :],
                                            psBC[:, 0, :],
                                            Op.subtract).then_inc(
                                P.sem("evV"), 1)
                            n += 1
                            v.wait_ge(P.s["evV"], n)
                            v.tensor_tensor(xT[:, cc, :], st1[:, :],
                                            psBC[:, 1, :], Op.mult).then_inc(
                                P.sem("evV"), 1)
                            n += 1
                            v.wait_ge(P.s["evV"], n)
                    P.on("vector", f)
                    xn_done = P.bump("evV", 2 * CH)

                P.dma("sync", "xs", dst_ap, xT[:, :, :],
                      waits=[("evV", xn_done)])

    def rope_qknorm(scr, sqb, pre_sb, pre_idx, rpt, out_sb, out_idx, is_q,
                    rwA, rwB, pre_done, tag, rpt_t=0):
        with contextlib.ExitStack() as PX:
            psSw = PX.enter_context(
                nc.psum_tensor(f"psSw{tag}", [128, 2, 512], F32))
            sw_d = {}
            for h4 in range(4):
                slot = h4 % 2

                def f(t, h4=h4, slot=slot, pre_done=pre_done,
                      e2=sw_d.get(h4 - 2, 0)):
                    t.wait_ge(P.s["evV"], pre_done)
                    if e2:
                        t.wait_ge(P.s["evV"], e2)
                    t.matmul(psSw[:, slot, :], cbf[:, oPS:oPS + 128],
                             pre_sb[:, pre_idx, h4 * 512:(h4 + 1) * 512],
                             start=True, stop=True).then_inc(P.sem("mmP"), 1)
                P.on("tensor", f)
                mm = P.bump("mmP")

                def f(v, mm=mm, slot=slot, h4=h4, rpt_t=rpt_t):
                    v.wait_ge(P.s["mmP"], mm)
                    if rpt_t:
                        v.wait_ge(P.s["xl"], rpt_t)
                    v.tensor_tensor(scr[:, 1, h4 * 512:(h4 + 1) * 512],
                                    psSw[:, slot, :],
                                    rpt[:, 1, h4 * 512:(h4 + 1) * 512],
                                    Op.mult).then_inc(P.sem("evV"), 1)
                P.on("vector", f)
                sw_d[h4] = P.bump("evV")

            def f(v, sd=sw_d[3], n0=P.cur("evV")):
                v.wait_ge(P.s["evV"], sd)
                v.tensor_tensor(scr[:, 0, :], scr[:, 0, :], rpt[:, 0, :],
                                Op.mult).then_inc(P.sem("evV"), 1)
                v.wait_ge(P.s["evV"], n0 + 1)
                v.tensor_tensor(scr[:, 0, :], scr[:, 0, :], scr[:, 1, :],
                                Op.add).then_inc(P.sem("evV"), 1)
                v.wait_ge(P.s["evV"], n0 + 2)
                v.tensor_tensor(sqb[:, 0, :], scr[:, 0, :], scr[:, 0, :],
                                Op.mult).then_inc(P.sem("evV"), 1)
            P.on("vector", f)
            roped = P.bump("evV", 3)

        with contextlib.ExitStack() as PX:
            psR = PX.enter_context(nc.psum_tensor(f"psR{tag}", [1, S], F32))
            row_done = 0
            for hh, row in ((0, rwA), (1, rwB)):
                def f(t, hh=hh, roped=roped, row_done=row_done):
                    t.wait_ge(P.s["evV"], roped)
                    if row_done:
                        t.wait_ge(P.s["evV"], row_done)
                    for q4 in range(4):
                        inst = t.matmul(psR[0:1, q4 * 512:(q4 + 1) * 512],
                                        cbf[:, oHDs + hh:oHDs + hh + 1],
                                        sqb[:, 0, q4 * 512:(q4 + 1) * 512],
                                        start=True, stop=True)
                    inst.then_inc(P.sem("mmP"), 1)
                P.on("tensor", f)
                mm = P.bump("mmP")

                def f(v, mm=mm, row=row):
                    v.wait_ge(P.s["mmP"], mm)
                    v.tensor_scalar(row[0:1, :], psR[0:1, :], 1.0 / HD, None,
                                    Op.mult).then_inc(P.sem("evV"), 1)
                P.on("vector", f)
                row_done = P.bump("evV")

            def f(sc, row_done=row_done, n0=P.cur("evA")):
                sc.wait_ge(P.s["evV"], row_done)
                sc.activation(rwA[0:1, :], rwA[0:1, :], AF.Ln,
                              bias=ceps[0:1, 1:2]).then_inc(P.sem("evA"), 1)
                sc.activation(rwB[0:1, :], rwB[0:1, :], AF.Ln,
                              bias=ceps[0:1, 1:2]).then_inc(P.sem("evA"), 1)
                sc.wait_ge(P.s["evA"], n0 + 2)
                sc.activation(rwA[0:1, :], rwA[0:1, :], AF.Exp,
                              scale=-0.5).then_inc(P.sem("evA"), 1)
                sc.activation(rwB[0:1, :], rwB[0:1, :], AF.Exp,
                              scale=-0.5).then_inc(P.sem("evA"), 1)
            P.on("scalar", f)
            rr = P.bump("evA", 4)

        with contextlib.ExitStack() as PX:
            psBC = PX.enter_context(nc.psum_tensor(f"psBCq{tag}", [128, S], F32))
            selo = 384 if is_q else 128

            def f(t, rr=rr, selo=selo):
                t.wait_ge(P.s["evA"], rr)
                for q4 in range(4):
                    t.matmul(psBC[:, q4 * 512:(q4 + 1) * 512],
                             crows[0:1, selo:selo + 128],
                             rwA[0:1, q4 * 512:(q4 + 1) * 512],
                             start=True, stop=False)
                    inst = t.matmul(psBC[:, q4 * 512:(q4 + 1) * 512],
                                    crows[0:1, selo + 128:selo + 256],
                                    rwB[0:1, q4 * 512:(q4 + 1) * 512],
                                    start=False, stop=True)
                inst.then_inc(P.sem("mmP"), 1)
            P.on("tensor", f)
            bc2 = P.bump("mmP")

            def f(v, bc2=bc2):
                v.wait_ge(P.s["mmP"], bc2)
                v.tensor_tensor(out_sb[:, out_idx, :], scr[:, 0, :],
                                psBC[:, :], Op.mult).then_inc(P.sem("evV"), 1)
            P.on("vector", f)
            P.bump("evV")

    def sdpa_all(g, Q, KLD, VT, AO, Pstr, Zsb, Zall, rZ, psSc, psO, psZ,
                 psZbc, ready0):
        metas = []
        gctr = 0
        for hp in range(NH // 2):
            for jq in range(4):
                if g:
                    kts = list(range(0, 4 * (jq + 1)))
                else:
                    kts = list(range(max(0, 4 * jq - 2), 4 * jq + 4))
                groups = [kts[i:i + 2] for i in range(0, len(kts), 2)]
                ntot = len(kts)
                for gi, grp in enumerate(groups):
                    for h in (2 * hp, 2 * hp + 1):
                        moffs = []
                        for kt in grp:
                            d = kt - 4 * jq
                            if g:
                                moffs.append(oMG + 384 - 128 * d if d >= 0
                                             else None)
                            else:
                                moffs.append(oML + 384 - 128 * d)
                        metas.append(dict(
                            h=h, jq=jq, b0=64 * (h % 2), kvh=h // 4,
                            qch=h // 2, grp=tuple(grp), slot=gctr % 2,
                            moffs=tuple(moffs), ki0=2 * gi, ntot=ntot,
                            oslot=h % 2, zp=32 * (h % 2),
                            first=(gi == 0),
                            last=(2 * gi + len(grp) == ntot)))
                        gctr += 1
        NG = len(metas)
        expd = [0] * NG
        qkd = [0] * NG
        sd = dict(zraw={0: 0, 1: 0}, lnprev={0: 0, 1: 0}, zbcwar=0,
                  po={0: 0, 1: 0})
        pending = []

        def emit_qk(i):
            m = metas[i]
            exp_rel = expd[i - 2] if i >= 2 else 0
            rdy = ready0 if i == 0 else 0

            def f(t, m=m, exp_rel=exp_rel, rdy=rdy):
                if rdy:
                    t.wait_ge(P.s["evV"], rdy)
                if exp_rel:
                    t.wait_ge(P.s["evA"], exp_rel)
                qsl = Q[m["b0"]:m["b0"] + 64, m["qch"],
                        m["jq"] * 512:(m["jq"] + 1) * 512]
                for j, kt in enumerate(m["grp"]):
                    inst = t.matmul(psSc[:, m["slot"], j, :],
                                    KLD[m["b0"]:m["b0"] + 64, m["kvh"],
                                        kt * 128:(kt + 1) * 128],
                                    qsl, start=True, stop=True)
                inst.then_inc(P.sem("mmP"), 1)
            P.on("tensor", f)
            qkd[i] = P.bump("mmP")

        def make_head_tail(m, rec):
            def emit():
                zp = m["zp"]
                for j4 in range(4):
                    def f(t, rec=rec, j4=j4, war=sd["zbcwar"], zp=zp):
                        t.wait_ge(P.s["evA"], rec)
                        if war:
                            t.wait_ge(P.s["evV"], war)
                        if zp:
                            t.matmul(psZbc[:, :], crows32[32:33, 0:128],
                                     rZ[32:33, j4 * 512:(j4 + 1) * 512],
                                     start=True, stop=True).then_inc(
                                P.sem("mmP"), 1)
                        else:
                            t.matmul(psZbc[:, :], crows[0:1, 0:128],
                                     rZ[0:1, j4 * 512:(j4 + 1) * 512],
                                     start=True, stop=True).then_inc(
                                P.sem("mmP"), 1)
                    P.on("tensor", f)
                    bc = P.bump("mmP")

                    def f(v, bc=bc, b0=m["b0"], j4=j4):
                        v.wait_ge(P.s["mmP"], bc)
                        v.tensor_copy(Zsb[b0:b0 + 64, j4 % 2, :],
                                      psZbc[b0:b0 + 64, :]).then_inc(
                            P.sem("evV"), 1)
                    P.on("vector", f)
                    sd["zbcwar"] = P.bump("evV")

                    def f(v, zc=sd["zbcwar"], b0=m["b0"], qch=m["qch"],
                          j4=j4):
                        v.wait_ge(P.s["evV"], zc)
                        v.tensor_tensor(
                            AO[b0:b0 + 64, qch, j4 * 512:(j4 + 1) * 512],
                            AO[b0:b0 + 64, qch, j4 * 512:(j4 + 1) * 512],
                            Zsb[b0:b0 + 64, j4 % 2, :],
                            Op.mult).then_inc(P.sem("evV"), 1)
                    P.on("vector", f)
                    P.bump("evV")
            return emit

        emit_qk(0)
        for i in range(NG):
            m = metas[i]

            def f(sc, m=m, qk=qkd[i]):
                sc.wait_ge(P.s["mmP"], qk)
                sc.activation(Pstr[:, m["slot"], :], psSc[:, m["slot"], :, :],
                              AF.Exp, bias=ceps[:, 2:3]).then_inc(
                    P.sem("evA"), 1)
            P.on("scalar", f)
            expd[i] = P.bump("evA")

            if i + 1 < NG:
                emit_qk(i + 1)

            nmask = sum(1 for mo in m["moffs"] if mo is not None)
            if nmask:
                def f(v, m=m, e=expd[i]):
                    v.wait_ge(P.s["evA"], e)
                    for j, moff in enumerate(m["moffs"]):
                        if moff is None:
                            continue
                        v.scalar_tensor_tensor(
                            Pstr[:, m["slot"], j * 512:(j + 1) * 512],
                            Pstr[:, m["slot"], j * 512:(j + 1) * 512], 1.0,
                            cbf[:, moff:moff + 512], Op.mult,
                            Op.mult).then_inc(P.sem("evV"), 1)
                P.on("vector", f)
                prod = ("evV", P.bump("evV", nmask))
            else:
                prod = ("evA", expd[i])

            def f(t, m=m, prod=prod,
                  zwar=(sd["zraw"][m["oslot"]] if m["first"] else 0),
                  powar=(sd["po"][m["oslot"]] if m["first"] else 0)):
                t.wait_ge(P.s[prod[0]], prod[1])
                if zwar:
                    t.wait_ge(P.s["evV"], zwar)
                if powar:
                    t.wait_ge(P.s["evV"], powar)
                zp = m["zp"]
                for j in range(len(m["grp"])):
                    if zp:
                        t.matmul(psZ[32:33, :], cbf[:, oOC:oOC + 1],
                                 Pstr[:, m["slot"], j * 512:(j + 1) * 512],
                                 start=(m["ki0"] + j == 0),
                                 stop=(m["ki0"] + j == m["ntot"] - 1),
                                 tile_position=(0, 32))
                    else:
                        t.matmul(psZ[0:1, :], cbf[:, oOC:oOC + 1],
                                 Pstr[:, m["slot"], j * 512:(j + 1) * 512],
                                 start=(m["ki0"] + j == 0),
                                 stop=(m["ki0"] + j == m["ntot"] - 1))
                for j, kt in enumerate(m["grp"]):
                    inst = t.matmul(
                        psO[m["b0"]:m["b0"] + 64, m["oslot"], :],
                        VT[:, kt, m["kvh"] * 64:(m["kvh"] + 1) * 64],
                        Pstr[:, m["slot"], j * 512:(j + 1) * 512],
                        start=(m["ki0"] + j == 0),
                        stop=(m["ki0"] + j == m["ntot"] - 1),
                        tile_position=(0, m["b0"]))
                inst.then_inc(P.sem("mmP"), 1)
            P.on("tensor", f)
            pvz = P.bump("mmP")

            if m["last"]:
                def f(v, m=m, pvz=pvz,
                      lnw=(sd["lnprev"][m["oslot"]] if m["jq"] == 0 else 0)):
                    v.wait_ge(P.s["mmP"], pvz)
                    v.tensor_copy(
                        AO[m["b0"]:m["b0"] + 64, m["qch"],
                           m["jq"] * 512:(m["jq"] + 1) * 512],
                        psO[m["b0"]:m["b0"] + 64, m["oslot"], :]).then_inc(
                        P.sem("evV"), 1)
                    if lnw:
                        v.wait_ge(P.s["evA"], lnw)
                    v.tensor_copy(
                        Zall[m["zp"]:m["zp"] + 1,
                             m["jq"] * 512:(m["jq"] + 1) * 512],
                        psZ[m["zp"]:m["zp"] + 1, :]).then_inc(P.sem("evV"), 1)
                P.on("vector", f)
                sd["po"][m["oslot"]] = P.bump("evV")
                sd["zraw"][m["oslot"]] = P.bump("evV")

                if m["jq"] == 3:
                    def f(sc, zraw=sd["zraw"][m["oslot"]], zbw=sd["zbcwar"],
                          zp=m["zp"], n0=P.cur("evA")):
                        sc.wait_ge(P.s["evV"], zraw)
                        if zbw:
                            sc.wait_ge(P.s["evV"], zbw)
                        sc.activation(rZ[zp:zp + 1, :], Zall[zp:zp + 1, :],
                                      AF.Ln).then_inc(P.sem("evA"), 1)
                        sc.wait_ge(P.s["evA"], n0 + 1)
                        sc.activation(rZ[zp:zp + 1, :], rZ[zp:zp + 1, :],
                                      AF.Exp, scale=-1.0).then_inc(
                            P.sem("evA"), 1)
                    P.on("scalar", f)
                    rec = P.bump("evA", 2)
                    sd["lnprev"][m["oslot"]] = rec - 1
                    pending.append([i + 4, make_head_tail(m, rec)])

            for item in list(pending):
                if item[0] <= i:
                    item[1]()
                    pending.remove(item)

        for item in pending:
            item[1]()

    def attn_sublayer(sq, kind):
        g = kind == "g"
        wq_d = attw["gwq" if g else "lwq"]
        wo_d = attw["gwo" if g else "lwo"]
        kv_d = attw["gkvd" if g else "lwk"]
        nrm = 0 if g else 8
        rbase = 0 if g else 2
        res_src = tscr if g else xscr
        res_dst = tscr if g else xscr
        add_t = not g

        with contextlib.ExitStack() as SX:
            Q = SX.enter_context(nc.sbuf_tensor(f"Q{sq}{kind}", [128, CH, S], BF16))
            VT = SX.enter_context(nc.sbuf_tensor(f"VT{sq}{kind}", [128, 16, 256], BF16))
            KLD = SX.enter_context(nc.sbuf_tensor(f"KLD{sq}{kind}", [128, 4, S], BF16))

            with contextlib.ExitStack() as S1:
                XN = S1.enter_context(nc.sbuf_tensor(f"XN{sq}{kind}", [128, CH, S], BF16))
                KL = S1.enter_context(nc.sbuf_tensor(f"KL{sq}{kind}", [128, 2, S], BF16))
                LATB = S1.enter_context(nc.sbuf_tensor(f"LATB{sq}{kind}", [128, 2, S], BF16))
                wq_a = S1.enter_context(nc.sbuf_tensor(f"wqa{sq}{kind}", [128, 8, D], BF16))
                kvd_a = S1.enter_context(nc.sbuf_tensor(f"kvda{sq}{kind}", [128, 8, 256], BF16))
                wv_a = S1.enter_context(nc.sbuf_tensor(f"wva{sq}{kind}", [128, 8, 256], BF16))
                scr = S1.enter_context(nc.sbuf_tensor(f"scr{sq}{kind}", [128, 2, S], F32))
                sqb = S1.enter_context(nc.sbuf_tensor(f"sqb{sq}{kind}", [128, 2, S], BF16))
                rwA = S1.enter_context(nc.sbuf_tensor(f"rwAa{sq}{kind}", [1, S], F32))
                rwB = S1.enter_context(nc.sbuf_tensor(f"rwBa{sq}{kind}", [1, S], F32))
                rpt = S1.enter_context(nc.sbuf_tensor(f"rpt{sq}{kind}", [128, 2, S], F32))

                rpt_t = P.dma("sync", "xl", rpt[:, :, :],
                              ropeD[rbase:rbase + 2].rearrange("r p t -> p r t"))
                P.dma("gpsimd", "aw", wq_a[:, :, :],
                      wq_d.rearrange("(kc p) o -> p kc o", p=128),
                      max_dma_last_dim=4096)
                P.dma("gpsimd", "aw", kvd_a[:, :, :],
                      kv_d.rearrange("(kc p) o -> p kc o", p=128))
                if g:
                    P.dma("gpsimd", "aw", wv_a[:, 0:2, :],
                          attw["gkup"].rearrange("(kc p) o -> p kc o", p=128))
                    P.dma("gpsimd", "aw", wv_a[:, 2:4, :],
                          attw["gvup"].rearrange("(kc p) o -> p kc o", p=128))
                else:
                    P.dma("gpsimd", "aw", wv_a[:, :, :],
                          attw["lwv"].rearrange("(kc p) o -> p kc o", p=128))
                aw_t = P.cur("aw")

                with contextlib.ExitStack() as PX:
                    psB = PX.enter_context(nc.psum_tensor(f"psBr{sq}{kind}", [1, S], F32))
                    sq_d = {}
                    st_d = {}
                    for cc in range(CH):
                        sl = f"xlA{cc % 2}"
                        xl_t = P.dma("sync", sl, scr[:, cc % 2, :],
                                     tscr[sq, cc],
                                     waits=[("xs", P.cur("xs")),
                                            ("evV", sq_d.get(cc - 2, 0))])

                        def f(v, cc=cc, xl_t=xl_t, sl=sl,
                              sr=st_d.get(cc - 2, 0)):
                            v.wait_ge(P.s[sl], xl_t)
                            if sr:
                                v.wait_ge(P.s["mmP"], sr)
                            v.tensor_tensor(sqb[:, cc % 2, :], scr[:, cc % 2, :],
                                            scr[:, cc % 2, :],
                                            Op.mult).then_inc(P.sem("evV"), 1)
                        P.on("vector", f)
                        sq_d[cc] = P.bump("evV")

                        def f(t, cc=cc, sd=sq_d[cc]):
                            t.wait_ge(P.s["evV"], sd)
                            for q4 in range(4):
                                inst = t.matmul(
                                    psB[0:1, q4 * 512:(q4 + 1) * 512],
                                    cbf[:, oOC:oOC + 1],
                                    sqb[:, cc % 2, q4 * 512:(q4 + 1) * 512],
                                    start=(cc == 0), stop=(cc == CH - 1))
                            inst.then_inc(P.sem("mmP"), 1)
                        P.on("tensor", f)
                        st_d[cc] = P.bump("mmP")
                    stats_done = st_d[CH - 1]

                    def f(v, stats_done=stats_done):
                        v.wait_ge(P.s["mmP"], stats_done)
                        v.tensor_scalar(rwB[0:1, :], psB[0:1, :], 1.0 / D,
                                        None, Op.mult).then_inc(P.sem("evV"), 1)
                    P.on("vector", f)
                    msd = P.bump("evV")

                    def f(sc, msd=msd, n0=P.cur("evA")):
                        sc.wait_ge(P.s["evV"], msd)
                        sc.activation(rwB[0:1, :], rwB[0:1, :], AF.Ln,
                                      bias=ceps[0:1, 1:2]).then_inc(P.sem("evA"), 1)
                        sc.wait_ge(P.s["evA"], n0 + 1)
                        sc.activation(rwB[0:1, :], rwB[0:1, :], AF.Exp,
                                      scale=-0.5).then_inc(P.sem("evA"), 1)
                    P.on("scalar", f)
                    rstd_done = P.bump("evA", 2)

                with contextlib.ExitStack() as PX:
                    psBC = PX.enter_context(
                        nc.psum_tensor(f"psBCr{sq}{kind}", [128, S], F32))

                    def f(t, rstd_done=rstd_done):
                        t.wait_ge(P.s["evA"], rstd_done)
                        for q4 in range(4):
                            inst = t.matmul(psBC[:, q4 * 512:(q4 + 1) * 512],
                                            crows[0:1, 0:128],
                                            rwB[0:1, q4 * 512:(q4 + 1) * 512],
                                            start=True, stop=True)
                        inst.then_inc(P.sem("mmP"), 1)
                    P.on("tensor", f)
                    bcd = P.bump("mmP")

                    xn_d = {}
                    for cc in range(CH):
                        sl = f"xlA{cc % 2}"
                        xl_t = P.dma("sync", sl, scr[:, cc % 2, :],
                                     tscr[sq, cc],
                                     waits=[("evV", xn_d.get(cc - 2, 0) or
                                             P.cur("evV"))])

                        def f(v, cc=cc, xl_t=xl_t, bcd=bcd, sl=sl):
                            v.wait_ge(P.s[sl], xl_t)
                            v.wait_ge(P.s["mmP"], bcd)
                            v.scalar_tensor_tensor(
                                XN[:, cc, :], scr[:, cc % 2, :],
                                anw[:, nrm + cc:nrm + cc + 1], psBC[:, :],
                                Op.mult, Op.mult).then_inc(P.sem("evV"), 1)
                        P.on("vector", f)
                        xn_d[cc] = P.bump("evV")
                    xn_done = xn_d[CH - 1]

                with contextlib.ExitStack() as PX:
                    psP = PX.enter_context(
                        nc.psum_tensor(f"psP{sq}{kind}", [128, 2, 512], F32))
                    pg = dict(g=0, rel={0: 0, 1: 0})

                    def proj_group(lhs_fn, rhs_fn, nk, out_sl, extra=0):
                        slot = pg["g"] % 2
                        pg["g"] += 1

                        def f(t, slot=slot, nk=nk, lhs_fn=lhs_fn,
                              rhs_fn=rhs_fn, aw_t=aw_t, xn_done=xn_done,
                              e2=pg["rel"][slot], extra=extra):
                            t.wait_ge(P.s["aw"], aw_t)
                            t.wait_ge(P.s["evV"], xn_done)
                            if extra:
                                t.wait_ge(P.s["evV"], extra)
                            if e2:
                                t.wait_ge(P.s["evV"], e2)
                            for k in range(nk):
                                inst = t.matmul(psP[:, slot, :], lhs_fn(k),
                                                rhs_fn(k), start=(k == 0),
                                                stop=(k == nk - 1))
                            inst.then_inc(P.sem("mmP"), 1)
                        P.on("tensor", f)
                        mm = P.bump("mmP")

                        def f(v, mm=mm, slot=slot, out_sl=out_sl):
                            v.wait_ge(P.s["mmP"], mm)
                            v.tensor_copy(out_sl,
                                          psP[:, slot, :]).then_inc(
                                P.sem("evV"), 1)
                        P.on("vector", f)
                        ev = P.bump("evV")
                        pg["rel"][slot] = ev
                        return ev

                    lat_done = 0
                    if g:
                        for oc2 in range(2):
                            for h4 in range(4):
                                lat_done = proj_group(
                                    lambda k, oc2=oc2: kvd_a[:, k, oc2 * 128:(oc2 + 1) * 128],
                                    lambda k, h4=h4: XN[:, k, h4 * 512:(h4 + 1) * 512],
                                    CH, LATB[:, oc2, h4 * 512:(h4 + 1) * 512])

                    for lc in range(2):
                        for h4 in range(4):
                            if g:
                                kev = proj_group(
                                    lambda k, lc=lc: wv_a[:, k, lc * 128:(lc + 1) * 128],
                                    lambda k, h4=h4: LATB[:, k, h4 * 512:(h4 + 1) * 512],
                                    2, scr[:, 0, h4 * 512:(h4 + 1) * 512],
                                    extra=lat_done)
                            else:
                                kev = proj_group(
                                    lambda k, lc=lc: kvd_a[:, k, lc * 128:(lc + 1) * 128],
                                    lambda k, h4=h4: XN[:, k, h4 * 512:(h4 + 1) * 512],
                                    CH, scr[:, 0, h4 * 512:(h4 + 1) * 512])

                        def f(v, kev=kev, lc=lc):
                            v.wait_ge(P.s["evV"], kev)
                            v.tensor_copy(KL[:, lc, :], scr[:, 0, :]).then_inc(
                                P.sem("evV"), 1)
                        P.on("vector", f)
                        pre_done = P.bump("evV")
                        rope_qknorm(scr, sqb, KL, lc, rpt, KL, lc, False,
                                    rwA, rwB, pre_done, f"k{sq}{kind}{lc}",
                                    rpt_t=rpt_t)

                    with contextlib.ExitStack() as P2:
                        psV = P2.enter_context(
                            nc.psum_tensor(f"psV{sq}{kind}", [128, 2, 512], F32))
                        v_d = {}
                        for tt_ in range(16):
                            slot = tt_ % 2

                            def f(t, tt_=tt_, slot=slot, lat_done=lat_done,
                                  aw_t=aw_t, e2=v_d.get(tt_ - 2, 0),
                                  evv0=P.cur("evV")):
                                if lat_done:
                                    t.wait_ge(P.s["evV"], lat_done)
                                t.wait_ge(P.s["evV"], evv0)
                                t.wait_ge(P.s["aw"], aw_t)
                                if e2:
                                    t.wait_ge(P.s["evV"], e2)
                                if g:
                                    for k in range(2):
                                        inst = t.matmul(
                                            psV[:, slot, 0:256],
                                            LATB[:, k, tt_ * 128:(tt_ + 1) * 128],
                                            wv_a[:, 2 + k, :],
                                            start=(k == 0), stop=(k == 1))
                                else:
                                    for k in range(CH):
                                        inst = t.matmul(
                                            psV[:, slot, 0:256],
                                            XN[:, k, tt_ * 128:(tt_ + 1) * 128],
                                            wv_a[:, k, :],
                                            start=(k == 0), stop=(k == CH - 1))
                                inst.then_inc(P.sem("mmP"), 1)
                            P.on("tensor", f)
                            mm = P.bump("mmP")

                            def f(v, mm=mm, slot=slot, tt_=tt_):
                                v.wait_ge(P.s["mmP"], mm)
                                v.tensor_copy(VT[:, tt_, :],
                                              psV[:, slot, 0:256]).then_inc(
                                    P.sem("evV"), 1)
                            P.on("vector", f)
                            v_d[tt_] = P.bump("evV")

                    for qc in range(CH):
                        for h4 in range(4):
                            qev = proj_group(
                                lambda k, qc=qc: wq_a[:, k, qc * 128:(qc + 1) * 128],
                                lambda k, h4=h4: XN[:, k, h4 * 512:(h4 + 1) * 512],
                                CH, scr[:, 0, h4 * 512:(h4 + 1) * 512])

                        def f(v, qev=qev, qc=qc):
                            v.wait_ge(P.s["evV"], qev)
                            v.tensor_copy(Q[:, qc, :], scr[:, 0, :]).then_inc(
                                P.sem("evV"), 1)
                        P.on("vector", f)
                        pre_done = P.bump("evV")
                        rope_qknorm(scr, sqb, Q, qc, rpt, Q, qc, True,
                                    rwA, rwB, pre_done, f"q{sq}{kind}{qc}",
                                    rpt_t=rpt_t)

                with contextlib.ExitStack() as P2:
                    psDp = P2.enter_context(
                        nc.psum_tensor(f"psDp{sq}{kind}", [128, 2, 512], F32))
                    d_rel = {0: 0, 1: 0}
                    gi2 = 0
                    for c4 in range(4):
                        for h4 in range(4):
                            slot = gi2 % 2
                            gi2 += 1

                            def f(t, c4=c4, h4=h4, slot=slot,
                                  e2=d_rel[slot], pe=P.cur("evV")):
                                t.wait_ge(P.s["evV"], pe)
                                if e2:
                                    t.wait_ge(P.s["evV"], e2)
                                t.matmul(psDp[:, slot, :],
                                         cbf[:, oPD + c4 * 128:oPD + (c4 + 1) * 128],
                                         KL[:, c4 // 2, h4 * 512:(h4 + 1) * 512],
                                         start=True, stop=True).then_inc(
                                    P.sem("mmP"), 1)
                            P.on("tensor", f)
                            mm = P.bump("mmP")

                            def f(v, mm=mm, slot=slot, c4=c4, h4=h4):
                                v.wait_ge(P.s["mmP"], mm)
                                v.tensor_copy(
                                    KLD[:, c4, h4 * 512:(h4 + 1) * 512],
                                    psDp[:, slot, :]).then_inc(P.sem("evV"), 1)
                            P.on("vector", f)
                            d_rel[slot] = P.bump("evV")

            P.barrier()

            with contextlib.ExitStack() as S2:
                AO = S2.enter_context(nc.sbuf_tensor(f"AO{sq}{kind}", [128, CH, S], BF16))
                wo_a = S2.enter_context(nc.sbuf_tensor(f"woa{sq}{kind}", [128, 8, D], BF16))
                Pstr = S2.enter_context(nc.sbuf_tensor(f"Pstr{sq}{kind}", [128, 2, 1024], BF16))
                Zsb = S2.enter_context(nc.sbuf_tensor(f"Zsb{sq}{kind}", [128, 2, 512], F32))
                Zall = S2.enter_context(nc.sbuf_tensor(f"Zall{sq}{kind}", [33, 2048], F32))
                rZ = S2.enter_context(nc.sbuf_tensor(f"rZ{sq}{kind}", [33, 2048], F32))
                xc2 = S2.enter_context(nc.sbuf_tensor(f"xc2{sq}{kind}", [128, 2, S], F32))
                if add_t:
                    tc2 = S2.enter_context(nc.sbuf_tensor(f"tc2{sq}{kind}", [128, 2, S], F32))

                P.dma("gpsimd", "aw", wo_a[:, :, :],
                      wo_d.rearrange("(kc p) o -> p kc o", p=128),
                      max_dma_last_dim=4096)
                wo_t = P.cur("aw")

                with contextlib.ExitStack() as P2:
                    psSc = P2.enter_context(
                        nc.psum_tensor(f"psSc{sq}{kind}", [128, 2, 2, 512], F32))
                    psO = P2.enter_context(
                        nc.psum_tensor(f"psO{sq}{kind}", [128, 2, 512], F32))
                    psZ = P2.enter_context(
                        nc.psum_tensor(f"psZ{sq}{kind}", [33, 512], F32))
                    psZbc = P2.enter_context(
                        nc.psum_tensor(f"psZbc{sq}{kind}", [128, 512], F32))
                    sdpa_all(g, Q, KLD, VT, AO, Pstr, Zsb, Zall, rZ,
                             psSc, psO, psZ, psZbc, P.cur("evV"))

                ao_done = P.cur("evV")
                with contextlib.ExitStack() as P2:
                    psP2 = P2.enter_context(
                        nc.psum_tensor(f"psP2{sq}{kind}", [128, 2, 512], F32))
                    orel = {0: 0, 1: 0}
                    tadd_d = {}
                    gi3 = 0
                    for oc in range(CH):
                        sl = f"xlB{oc % 2}"
                        xl_t = P.dma("sync", sl, xc2[:, oc % 2, :],
                                     res_src[sq, oc],
                                     waits=[("xs", P.cur("xs")),
                                            ("evV", orel[oc % 2])])
                        if add_t:
                            sl2 = f"xlC{oc % 2}"
                            t2_t = P.dma("gpsimd", sl2, tc2[:, oc % 2, :],
                                         tscr[sq, oc],
                                         waits=[("xs", P.cur("xs")),
                                                ("evV", tadd_d.get(oc - 2, 0))])

                            def f(v, oc=oc, xl_t=xl_t, t2_t=t2_t, sl=sl,
                                  sl2=sl2):
                                v.wait_ge(P.s[sl], xl_t)
                                v.wait_ge(P.s[sl2], t2_t)
                                v.tensor_tensor(xc2[:, oc % 2, :],
                                                xc2[:, oc % 2, :],
                                                tc2[:, oc % 2, :],
                                                Op.add).then_inc(P.sem("evV"), 1)
                            P.on("vector", f)
                            tadd_d[oc] = P.bump("evV")
                            base_ev = tadd_d[oc]
                        else:
                            base_ev = 0
                        res_ev = 0
                        for h4 in range(4):
                            slot = gi3 % 2
                            gi3 += 1

                            def f(t, oc=oc, h4=h4, slot=slot, wo_t=wo_t,
                                  ao_done=ao_done, e2=orel[slot]):
                                t.wait_ge(P.s["aw"], wo_t)
                                t.wait_ge(P.s["evV"], ao_done)
                                if e2:
                                    t.wait_ge(P.s["evV"], e2)
                                for k in range(CH):
                                    inst = t.matmul(
                                        psP2[:, slot, :],
                                        wo_a[:, k, oc * 128:(oc + 1) * 128],
                                        AO[:, k, h4 * 512:(h4 + 1) * 512],
                                        start=(k == 0), stop=(k == CH - 1))
                                inst.then_inc(P.sem("mmP"), 1)
                            P.on("tensor", f)
                            mm = P.bump("mmP")

                            def f(v, mm=mm, slot=slot, oc=oc, h4=h4,
                                  xl_t=xl_t, sl=sl, base_ev=base_ev):
                                v.wait_ge(P.s["mmP"], mm)
                                v.wait_ge(P.s[sl], xl_t)
                                if base_ev:
                                    v.wait_ge(P.s["evV"], base_ev)
                                v.tensor_tensor(
                                    xc2[:, oc % 2, h4 * 512:(h4 + 1) * 512],
                                    psP2[:, slot, :],
                                    xc2[:, oc % 2, h4 * 512:(h4 + 1) * 512],
                                    Op.add).then_inc(P.sem("evV"), 1)
                            P.on("vector", f)
                            res_ev = P.bump("evV")
                            orel[slot] = res_ev
                        P.dma("sync", "xs", res_dst[sq, oc], xc2[:, oc % 2, :],
                              waits=[("evV", res_ev)])
            P.barrier()

    if cfg["ffn1"]:
        ffn_phase(x0, xscr, "f1")
        P.barrier()
    else:
        for sq in range(nseq):
            for cc in range(CH):
                P.dma("sync", "xs", xscr[sq, cc], x0[sq, cc])
        P.barrier()
    if cfg["attn"]:
        ln_phase(xscr, tscr, "ln")
        P.barrier()
        for sq in range(nseq):
            attn_sublayer(sq, "g")
            attn_sublayer(sq, "l")
    if cfg["ffn2"]:
        ffn_phase(xscr, y, "f2")
        snap = [(n, P.c[n]) for n in list(P.c) if P.c[n] > 0]

        def f(e, snap=tuple(snap)):
            for n, v in snap:
                e.wait_ge(P.s[n], v)
        P.on("sync", f)
    else:
        for sq in range(nseq):
            for cc in range(CH):
                P.dma("sync", "ys", y[sq, cc], xscr[sq, cc],
                      waits=[("xs", P.cur("xs"))])

        def f(e, t=P.cur("ys")):
            e.wait_ge(P.s["ys"], t)
        P.on("sync", f)

    CTX.close()
    return nc



_NC_CACHE = {}


def _get_nc():
    if "nc" not in _NC_CACHE:
        _NC_CACHE["nc"] = build_nc(CFG)
    return _NC_CACHE["nc"]


def kernel(**inputs):
    hidden = inputs["L0_attn_hidden"]
    W = _host_weights(inputs)
    C = _host_consts()
    nc = _get_nc()

    shared = dict(
        w12p=W["w12p"], h1p=W["h1p"], h2p=W["h2p"], w3p=W["w3p"],
        fsc=W["fsc"], anw=W["anw"], cbf=C["cbf"], crows=C["crows"],
        ropetab=C["ropetab"], ceps=C["ceps"], gwq=W["gwq"], gkvd=W["gkvd"], gkup=W["gkup"],
        gvup=W["gvup"], gwo=W["gwo"], lwq=W["lwq"], lwk=W["lwk"],
        lwv=W["lwv"], lwo=W["lwo"])

    in_maps = []
    for c in range(8):
        sl = hidden[c * NSEQ:(c + 1) * NSEQ]
        xfm = np.ascontiguousarray(sl.transpose(0, 2, 1)).reshape(
            NSEQ, CH, 128, S)
        m = dict(shared)
        m["x0"] = xfm
        in_maps.append(m)

    res = run_bass_kernel_spmd(nc, in_maps, core_ids=list(range(8)))
    outs = []
    for c in range(8):
        yc = res.results[c]["y"].reshape(NSEQ, D, S)
        outs.append(yc.transpose(0, 2, 1))
    return np.ascontiguousarray(np.concatenate(outs, axis=0))
